# version 06
# baseline (speedup 1.0000x reference)
import os
import math
import numpy as np
import ml_dtypes
import concourse.bass as bass
import concourse.mybir as mybir
from concourse.bass_utils import run_bass_kernel_spmd

F32 = mybir.dt.float32
BF16 = mybir.dt.bfloat16
AF = mybir.ActivationFunctionType
ALU = mybir.AluOpType
AX = mybir.AxisListType

STAGE = int(os.environ.get("MK_STAGE", "4"))
SUB = os.environ.get("MK_SUB", "z")

D = 4096
CH = 32
TH = 512
NTOK = 1024
NH = 32
DH = 128
EPS = 1e-5
GW = 3
NSLOT = 4

R_MEAN = 0
R_A = 512
R_B = 1024
R_SC = 1536
R_QM = [2048, 2080]
R_KM = 2112
R_KA = 2116
R_BD = 2120
R_BD2 = 2152
R_R = 2560
R_DEF = [3072, 3584]


def build_nc():
    nc = bass.Bass("TRN2", target_bir_lowering=False,
                   detect_race_conditions=False)

    def dt_in(name, shape, dt=F32):
        return nc.dram_tensor(name, shape, dt, kind="ExternalInput")

    xin = dt_in("xin", [D * NTOK])
    wupT = dt_in("wupT", [D, D])
    l0aux = dt_in("l0aux", [D, 128])
    exb0 = dt_in("exb0", [4])
    lb0T = dt_in("lb0T", [128, D])
    wq = dt_in("wq", [D, D])
    bqT = dt_in("bqT", [128, 32])
    wk = dt_in("wk", [D, DH])
    bk = dt_in("bk", [128])
    wv = dt_in("wv", [D, DH])
    bv = dt_in("bv", [128])
    wo = dt_in("wo", [D, D])
    boT = dt_in("boT", [128, 32])
    bwT = dt_in("bwT", [D, D])
    g1T = dt_in("g1T", [D, 128])
    la1T = dt_in("la1T", [D, 256])
    lb1T = dt_in("lb1T", [256, D])
    lamd = dt_in("lam", [64, 2])
    pri = dt_in("pri", [64, 2])
    qri = dt_in("qri", [64, 2])
    vcreT = dt_in("vcreT", [64, 64])
    vcimT = dt_in("vcimT", [64, 64])
    bpd = dt_in("bp", [64, 1])
    ctri = dt_in("ctri", [64, 2])
    dD = dt_in("dD", [1])
    logstep = dt_in("logstep", [1])
    identf = dt_in("identf", [128, 128])
    onesb = dt_in("onesb", [128, 128], BF16)
    exp4 = dt_in("exp4", [4, 32])
    exp28a = dt_in("exp28a", [28, 128])
    exp28b = dt_in("exp28b", [28, 128])

    out = nc.dram_tensor("out", [D * NTOK], F32, kind="ExternalOutput")
    kpad = nc.dram_tensor("kpad", [8192], BF16)
    krep = nc.dram_tensor("krep", [128 * 8193], BF16)
    qspill = nc.dram_tensor("qspill", [NH * DH * NTOK], BF16)
    xspill = nc.dram_tensor("xspill", [2 * D * TH], F32)
    hrows = nc.dram_tensor("hrows", [64 * 512], F32)
    arf = nc.dram_tensor("arf", [2 * 4096], F32)
    ksums = nc.dram_tensor("ksums", [16 * 4096], F32)
    ccin = nc.dram_tensor("ccin", [2 * 128 * NTOK], BF16)
    ccout = nc.dram_tensor("ccout", [4 * 128 * NTOK], BF16)

    guards = []

    def sb(name, shape, dt=F32):
        t = nc.sbuf_tensor(name, shape, dt)
        h = t.__enter__()
        guards.append(t)
        return h

    XH = sb("XH", [128, CH * TH])
    PING = sb("PING", [128, CH * TH], BF16)
    PONG = sb("PONG", [128, CH * TH], BF16)
    KFM = sb("KFM", [128, 2048], BF16)
    VTM = sb("VTM", [128, 2048], BF16)
    WR = sb("WR", [128, 2 * 3072], BF16)
    TR = sb("TR", [128, 2 * 3072 + 2 * 512], BF16)
    SCF = sb("SCF", [128, 2048])
    SCB = sb("SCB", [128, 8 * 512], BF16)
    PT = sb("PT", [128, 4 * 512], BF16)
    QHB = sb("QHB", [128, 2 * 512], BF16)
    ABC = sb("ABC", [128, 512])
    BBC = sb("BBC", [128, 512])
    ROW = sb("ROW", [1, 512])
    RDEF = sb("RDEF", [128, 1024])
    QN = sb("QN", [128, 96])
    GT = sb("GT", [128, 2 * 512], BF16)
    H224 = sb("H224", [128, 2 * 512], BF16)
    NEGB = sb("NEGB", [128, 32])
    SMF = sb("SMF", [128, 64])
    CONS = sb("CONS", [128, 128], BF16)
    CONF = sb("CONF", [128, 480])
    LAMS = sb("LAMS", [64, 16])
    VST = sb("VST", [64, 8])
    OMB = sb("OMB", [128, 8 * 32])
    W64 = sb("W64", [64, 6 * 64])
    ARL = sb("ARL", [128, 16 * 32])
    S64 = sb("S64", [64, 8 * 64])
    KSB = sb("KSB", [64, 64])
    KTB = sb("KTB", [64, 64], BF16)

    def psb(name):
        t = nc.psum_tensor(name, [128, 512], F32)
        h = t.__enter__()
        guards.append(t)
        return h

    PS = [psb(f"PS{i}") for i in range(8)]

    PINGF = PING[:, :].bitcast(F32)
    PONGF = PONG[:, :].bitcast(F32)

    ONES = CONS[:, 0:128]
    C_E4 = CONF[0:4, 192:224]
    C_E28A = CONF[0:28, 224:352]
    C_E28B = CONF[0:28, 352:480]

    class Sem:
        def __init__(self, name):
            g = nc.semaphore(name)
            self.h = g.__enter__()
            guards.append(g)
            self.n = 0

        def inc(self, k=1):
            self.n += k
            return self.n

    swd = Sem("s_wd")
    std = Sem("s_td")
    sxd = Sem("s_xd")
    sqd = Sem("s_qd")
    sbc = Sem("s_bc")
    smm = Sem("s_mm")
    sev = Sem("s_ev")
    svv = Sem("s_vv")
    ssc = Sem("s_sc")
    sgp = Sem("s_gp")
    scc = Sem("s_cc")
    sst = Sem("s_st")
    sod = Sem("s_od")
    ssu = Sem("s_su")
    swu = Sem("s_wu")
    stu = Sem("s_tu")
    swsl = [Sem(f"s_ws{i}") for i in range(NSLOT)]
    stsl = [Sem(f"s_ts{i}") for i in range(NSLOT)]
    sqsl = [Sem(f"s_qs{i}") for i in range(2)]
    sxsl = [Sem(f"s_xs{i}") for i in range(NSLOT)]

    blk = nc.Block()
    block = blk.__enter__()

    class Prog:
        def __init__(self):
            self.q = {k: [] for k in ("tensor", "vector", "scalar", "gpsimd", "sync")}

        def on(self, eng, f):
            self.q[eng].append(f)

        def flush(self):
            for e in ("sync", "gpsimd", "tensor", "vector", "scalar"):
                fs = self.q[e]
                if fs:
                    def runner(obj, fs=tuple(fs)):
                        for f in fs:
                            f(obj)
                    getattr(block, e)(runner)
                    self.q[e] = []

    P = Prog()
    ENGS = ("tensor", "vector", "scalar", "gpsimd", "sync")

    BENGS = ("tensor", "vector", "scalar", "sync")

    def barrier():
        for e in BENGS:
            if e == "sync":
                P.on(e, lambda o: o.nop().then_inc(sst.h, 1))
            else:
                P.on(e, lambda o: o.drain().then_inc(sst.h, 1))
        sst.n += 4
        tgt = sst.n
        for e in BENGS:
            P.on(e, lambda o, t=tgt: o.wait_ge(sst.h, t))
        P.flush()

    hslot = [0]

    def bcast_rows(pairs, wait_sem=None, wait_val=None):
        base = sbc.n
        n = len(pairs)
        slots = [16 + (hslot[0] + i) % 48 for i in range(n)]
        hslot[0] += n
        sbc.n += 32 * n
        done = sbc.n

        def go(o):
            o.wait_ge(sbc.h, base)
            if wait_sem is not None:
                o.wait_ge(wait_sem, wait_val)
            for (src, dst, w), sl in zip(pairs, slots):
                o.dma_start(out=bass.AP(hrows, sl * 512, [[1, w]]),
                            in_=src).then_inc(sbc.h, 16)
            o.wait_ge(sbc.h, base + 16 * n)
            for (src, dst, w), sl in zip(pairs, slots):
                o.dma_start(out=dst,
                            in_=bass.AP(hrows, sl * 512, [[0, 128], [1, w]])
                            ).then_inc(sbc.h, 16)
        P.on("sync", go)
        return done

    def chunk(buf, c):
        return buf[:, c * TH:(c + 1) * TH]

    def rw(off, w=512):
        return ROW[0:1, off:off + w]

    def prologue():
        def load_consts(o):
            o.dma_start(out=CONF[:, 0:128], in_=identf[:, :]).then_inc(sxd.h, 16)
            o.dma_start(out=CONF[:, 128:160], in_=bqT[:, :]).then_inc(sxd.h, 16)
            o.dma_start(out=CONF[:, 160:192], in_=boT[:, :]).then_inc(sxd.h, 16)
            o.dma_start(out=ONES, in_=onesb[:, :]).then_inc(sxd.h, 16)
            o.dma_start(out=C_E4, in_=exp4[:, :]).then_inc(sxd.h, 16)
            o.dma_start(out=C_E28A, in_=exp28a[:, :]).then_inc(sxd.h, 16)
            o.dma_start(out=C_E28B, in_=exp28b[:, :]).then_inc(sxd.h, 16)
            o.dma_start(out=LAMS[:, 0:2], in_=lamd[:, :]).then_inc(sxd.h, 16)
            o.dma_start(out=LAMS[:, 2:4], in_=ctri[:, :]).then_inc(sxd.h, 16)
            o.dma_start(out=LAMS[:, 4:6], in_=pri[:, :]).then_inc(sxd.h, 16)
            o.dma_start(out=LAMS[:, 6:8], in_=qri[:, :]).then_inc(sxd.h, 16)
            o.dma_start(out=LAMS[:, 8:9], in_=bpd[:, :]).then_inc(sxd.h, 16)
            o.dma_start(out=SMF[0:1, 0:1], in_=dD[None, :]).then_inc(sxd.h, 16)
            o.dma_start(out=SMF[0:1, 1:2], in_=logstep[None, :]).then_inc(sxd.h, 16)
            o.dma_start(out=SMF[:, 32:33],
                        in_=bass.AP(bk, 0, [[1, 128], [1, 1]])).then_inc(sxd.h, 16)
            o.dma_start(out=SMF[:, 33:34],
                        in_=bass.AP(bv, 0, [[1, 128], [1, 1]])).then_inc(sxd.h, 16)
            o.dma_start(out=SMF[32:36, 40:41],
                        in_=bass.AP(exb0, 0, [[1, 4], [1, 1]])).then_inc(sxd.h, 16)
            o.dma_start(out=S64[:, 0:64], in_=vcreT[:, :]).then_inc(sxd.h, 16)
            o.dma_start(out=S64[:, 64:128], in_=vcimT[:, :]).then_inc(sxd.h, 16)
        sxd.n += 19 * 16
        consts_done = sxd.n
        P.on("sync", load_consts)

        def v_init(o):
            o.memset(ROW[:, :], 0.0)
            o.memset(RDEF[:, :], 0.0)
            o.memset(GT[:, :], 0.0)
            o.memset(H224[:, :], 0.0)
        P.on("vector", v_init)

        IOT, OMS, OMT = OMB[:, 0:32], OMB[:, 32:64], OMB[:, 64:96]
        SH, SF, OC = OMB[:, 96:128], OMB[:, 128:160], OMB[:, 160:192]
        GRE, GIM = OMB[:, 192:224], OMB[:, 224:256]

        P.on("gpsimd", lambda o: o.iota(IOT, [[1, 32]], channel_multiplier=32,
                                        allow_small_or_imprecise_dtypes=True
                                        ).then_inc(sgp.h, 1))
        g1 = sgp.inc()

        def v_theta(o, g1=g1):
            o.wait_ge(sgp.h, g1)
            o.tensor_single_scalar(OMS, IOT, 2048.0, ALU.is_ge)
            o.drain()
            o.scalar_tensor_tensor(OMT, OMS, -4096.0, IOT, ALU.mult, ALU.add
                                   ).then_inc(svv.h, 1)
        v1 = svv.inc()
        P.on("vector", v_theta)

        def s_trig(o, v1=v1, cd=consts_done):
            o.wait_ge(svv.h, v1)
            o.activation(SH, OMT, AF.Sin, scale=float(math.pi / 4096.0))
            o.activation(SF, OMT, AF.Sin, scale=float(2.0 * math.pi / 4096.0))
            o.wait_ge(sxd.h, cd)
            o.activation(SMF[0:1, 2:3], SMF[0:1, 1:2], AF.Exp).then_inc(ssc.h, 1)
        s1 = ssc.inc()
        P.on("scalar", s_trig)

        def v_gc(o, s1=s1):
            o.wait_ge(ssc.h, s1)
            o.tensor_copy(PONGF[:, 4096:4128], SH)
            o.tensor_copy(PONGF[:, 4128:4160], SF)
            o.tensor_copy(PONGF[:, 4160:4192], OMT)
            o.tensor_copy(PONGF[:, 4192:4224], IOT)
            o.reciprocal(SMF[0:1, 3:4], SMF[0:1, 2:3])
            o.drain()
            o.tensor_single_scalar(SMF[0:1, 3:4], SMF[0:1, 3:4], 2.0, ALU.mult)
            o.tensor_mul(OC, SH, SH)
            o.drain()
            o.tensor_scalar(OC, OC, -2.0, 1.0, ALU.mult, ALU.add)
            o.drain()
            o.tensor_scalar(OMS, OC, 1.0, None, ALU.add)
            o.drain()
            o.tensor_mul(OMS, OMS, OMS)
            o.tensor_mul(OMT, SF, SF)
            o.drain()
            o.tensor_add(OMS, OMS, OMT)
            o.drain()
            o.tensor_single_scalar(OMS, OMS, 1e-38, ALU.add)
            o.tensor_scalar(GRE, OC, -1.0, 1.0, ALU.mult, ALU.add)
            o.tensor_scalar(GIM, OC, 1.0, None, ALU.add)
            o.drain()
            o.tensor_mul(GRE, GRE, GIM)
            o.drain()
            o.tensor_sub(GRE, GRE, OMT)
            o.reciprocal(OMS, OMS)
            o.drain()
            o.tensor_mul(GRE, GRE, OMS)
            o.tensor_scalar(GIM, SF, 2.0, None, ALU.mult)
            o.drain()
            o.tensor_mul(GIM, GIM, OMS).then_inc(svv.h, 1)
        v2 = svv.inc()
        P.on("vector", v_gc)

        def tosb(o, v2=v2, base=sbc.n):
            o.wait_ge(svv.h, v2)
            o.dma_start(out=bass.AP(hrows, 15 * 512, [[1, 1]]),
                        in_=SMF[0:1, 3:4]).then_inc(sbc.h, 16)
            o.wait_ge(sbc.h, base + 16)
            o.dma_start(out=SMF[:, 4:5],
                        in_=bass.AP(hrows, 15 * 512, [[0, 128], [1, 1]])
                        ).then_inc(sbc.h, 16)
        sbc.n += 32
        tos_done = sbc.n
        P.on("sync", tosb)

        def v_gc2(o, t=tos_done):
            o.wait_ge(sbc.h, t)
            o.tensor_single_scalar(GRE, GRE, SMF[:, 4:5], ALU.mult)
            o.tensor_single_scalar(GIM, GIM, SMF[:, 4:5], ALU.mult)
            o.tensor_scalar(SH, OC, 1.0, None, ALU.add)
            o.drain()
            o.tensor_scalar(SH, SH, 2.0, None, ALU.mult)
            o.drain()
            o.tensor_mul(SH, SH, OMS)
            o.tensor_scalar(SF, SF, 2.0, None, ALU.mult)
            o.drain()
            o.tensor_mul(SF, SF, OMS).then_inc(svv.h, 1)
        v3 = svv.inc()
        P.on("vector", v_gc2)
        CRE, CIM = SH, SF

        def g_hbm(o, v3=v3):
            o.wait_ge(svv.h, v3)
            o.dma_start(out=bass.AP(hrows, 0, [[32, 128], [1, 32]]),
                        in_=GRE).then_inc(sbc.h, 16)
            o.dma_start(out=bass.AP(hrows, 4096, [[32, 128], [1, 32]]),
                        in_=GIM).then_inc(sbc.h, 16)
        sbc.n += 32
        ghbm_done = sbc.n
        P.on("sync", g_hbm)

        def t_B(o, cd=consts_done):
            o.wait_ge(sxd.h, cd)
            o.matmul(PS[0][0:64, 0:1], S64[:, 0:64], LAMS[:, 8:9],
                     start=True, stop=True)
            o.matmul(PS[0][0:64, 1:2], S64[:, 64:128], LAMS[:, 8:9],
                     start=True, stop=True).then_inc(smm.h, 1)
        m1 = smm.inc()
        P.on("tensor", t_B)

        def v_vst(o, m1=m1):
            o.wait_ge(smm.h, m1)
            cre, cim = LAMS[:, 2:3], LAMS[:, 3:4]
            pre, pim = LAMS[:, 4:5], LAMS[:, 5:6]
            qre, qim = LAMS[:, 6:7], LAMS[:, 7:8]
            bre, bim = PS[0][0:64, 0:1], PS[0][0:64, 1:2]
            last = None
            for i, (ar, ai, br, bi) in enumerate(
                    [(cre, cim, bre, bim), (cre, cim, pre, pim),
                     (qre, qim, bre, bim), (qre, qim, pre, pim)]):
                o.tensor_mul(VST[:, i:i + 1], ar, br)
                o.tensor_mul(LAMS[:, 9:10], ai, bi)
                o.drain()
                o.tensor_add(VST[:, i:i + 1], VST[:, i:i + 1], LAMS[:, 9:10])
                o.tensor_mul(VST[:, 4 + i:5 + i], ar, bi)
                o.drain()
                o.tensor_mul(LAMS[:, 9:10], ai, br)
                o.drain()
                last = o.tensor_sub(VST[:, 4 + i:5 + i], VST[:, 4 + i:5 + i],
                                    LAMS[:, 9:10])
                o.drain()
            last.then_inc(svv.h, 1)
        v4 = svv.inc()
        P.on("vector", v_vst)

        GREB = PINGF[0:64, 0:4096]
        GIMB = PINGF[0:64, 4096:8192]

        def gbc(o, t=ghbm_done):
            o.wait_ge(sbc.h, t)
            o.dma_start(out=GREB, in_=bass.AP(hrows, 0, [[0, 64], [1, 4096]])
                        ).then_inc(sbc.h, 16)
            o.dma_start(out=GIMB, in_=bass.AP(hrows, 4096, [[0, 64], [1, 4096]])
                        ).then_inc(sbc.h, 16)
        sbc.n += 32
        gbc_done = sbc.n
        P.on("sync", gbc)

        def v_inv(o, t=gbc_done):
            o.wait_ge(sbc.h, t)
            o.tensor_single_scalar(GREB, GREB, LAMS[:, 0:1], ALU.subtract)
            o.tensor_single_scalar(GIMB, GIMB, LAMS[:, 1:2], ALU.subtract)
            o.drain()
            last = None
            for it in range(8):
                dre = GREB[:, it * 512:(it + 1) * 512]
                dim = GIMB[:, it * 512:(it + 1) * 512]
                mag = ABC[0:64, :]
                o.tensor_mul(mag, dre, dre)
                o.tensor_mul(BBC[0:64, :], dim, dim)
                o.drain()
                o.tensor_add(mag, mag, BBC[0:64, :])
                o.drain()
                o.reciprocal(mag, mag)
                o.drain()
                o.tensor_mul(dre, dre, mag)
                o.tensor_mul(dim, dim, mag)
                o.drain()
                last = o.tensor_scalar(dim, dim, -1.0, None, ALU.mult)
                o.drain()
            last.then_inc(svv.h, 1)
        v5 = svv.inc()
        P.on("vector", v_inv)
        IRE, IIM = GREB, GIMB

        mm_base = smm.n
        ev_base = sev.n

        def t_cauchy(o, v4=v4, v5=v5, ev_base=ev_base):
            o.wait_ge(svv.h, max(v4, v5))
            for it in range(8):
                if it >= 1:
                    o.wait_ge(sev.h, ev_base + it)
                o.matmul(PS[1][0:8, 0:512], VST[:, :],
                         IRE[:, it * 512:(it + 1) * 512], start=True, stop=True)
                o.matmul(PS[2][0:8, 0:512], VST[:, :],
                         IIM[:, it * 512:(it + 1) * 512], start=True, stop=True
                         ).then_inc(smm.h, 1)
        smm.n += 8
        P.on("tensor", t_cauchy)

        KSO = PONGF

        def v_cev(o, mb=mm_base):
            for it in range(8):
                o.wait_ge(smm.h, mb + it + 1)
                o.tensor_copy(KSO[0:8, it * 512:(it + 1) * 512], PS[1][0:8, 0:512])
                o.tensor_copy(KSO[32:40, it * 512:(it + 1) * 512], PS[2][0:8, 0:512]
                              ).then_inc(sev.h, 1)
        sev.n += 8
        cev_done = sev.n
        P.on("vector", v_cev)

        def ksb(o, t=cev_done, base=sbc.n):
            o.wait_ge(sev.h, t)
            o.dma_start(out=bass.AP(ksums, 0, [[4096, 8], [1, 4096]]),
                        in_=KSO[0:8, 0:4096]).then_inc(sbc.h, 16)
            o.dma_start(out=bass.AP(ksums, 8 * 4096, [[4096, 8], [1, 4096]]),
                        in_=KSO[32:40, 0:4096]).then_inc(sbc.h, 16)
            o.wait_ge(sbc.h, base + 32)
            o.dma_start(out=ARL[:, :],
                        in_=bass.AP(ksums, 0, [[32, 128], [4096, 16], [1, 32]])
                        ).then_inc(sbc.h, 16)
        sbc.n += 48
        ks_done = sbc.n
        P.on("sync", ksb)

        def blkc(i):
            return ARL[:, i * 32:(i + 1) * 32]

        def sc(i):
            return PINGF[:, i * 32:(i + 1) * 32]

        def v_ar(o, t=ks_done):
            o.wait_ge(sbc.h, t)
            for v in range(4):
                o.tensor_sub(sc(v), blkc(v), blkc(12 + v))
                o.tensor_add(sc(4 + v), blkc(8 + v), blkc(4 + v))
            o.drain()
            o.tensor_scalar(sc(8), sc(3), 1.0, None, ALU.add)
            o.drain()
            o.tensor_mul(sc(9), sc(8), sc(8))
            o.tensor_mul(sc(10), sc(7), sc(7))
            o.drain()
            o.tensor_add(sc(9), sc(9), sc(10))
            o.drain()
            o.reciprocal(sc(9), sc(9))
            o.drain()
            o.tensor_mul(sc(8), sc(8), sc(9))
            o.tensor_mul(sc(10), sc(7), sc(9))
            o.drain()
            o.tensor_scalar(sc(10), sc(10), -1.0, None, ALU.mult)
            o.tensor_mul(sc(11), sc(1), sc(8))
            o.drain()
            o.tensor_mul(sc(12), sc(5), sc(10))
            o.drain()
            o.tensor_sub(sc(11), sc(11), sc(12))
            o.tensor_mul(sc(12), sc(1), sc(10))
            o.tensor_mul(sc(13), sc(5), sc(8))
            o.drain()
            o.tensor_add(sc(12), sc(12), sc(13))
            o.tensor_mul(sc(13), sc(11), sc(2))
            o.drain()
            o.tensor_mul(sc(14), sc(12), sc(6))
            o.drain()
            o.tensor_sub(sc(13), sc(13), sc(14))
            o.tensor_mul(sc(14), sc(11), sc(6))
            o.tensor_mul(sc(15), sc(12), sc(2))
            o.drain()
            o.tensor_add(sc(14), sc(14), sc(15))
            o.tensor_sub(sc(13), sc(0), sc(13))
            o.drain()
            o.tensor_sub(sc(14), sc(4), sc(14))
            o.tensor_mul(sc(15), CRE, sc(13))
            o.drain()
            o.tensor_mul(sc(16), CIM, sc(14))
            o.drain()
            o.tensor_sub(sc(15), sc(15), sc(16))
            o.tensor_mul(sc(16), CRE, sc(14))
            o.tensor_mul(sc(17), CIM, sc(13))
            o.drain()
            o.tensor_add(sc(16), sc(16), sc(17)).then_inc(svv.h, 1)
        v6 = svv.inc()
        P.on("vector", v_ar)

        def arb(o, t=v6, base=sbc.n):
            o.wait_ge(svv.h, t)
            o.dma_start(out=bass.AP(arf, 0, [[32, 128], [1, 32]]),
                        in_=sc(15)).then_inc(sbc.h, 16)
            o.dma_start(out=bass.AP(arf, 4096, [[32, 128], [1, 32]]),
                        in_=sc(16)).then_inc(sbc.h, 16)
            o.wait_ge(sbc.h, base + 32)
            o.dma_start(out=S64[:, 128:192],
                        in_=bass.AP(arf, 0, [[64, 64], [1, 64]])).then_inc(sbc.h, 16)
            o.dma_start(out=S64[:, 192:256],
                        in_=bass.AP(arf, 4096, [[64, 64], [1, 64]])).then_inc(sbc.h, 16)
        sbc.n += 64
        ar_done = sbc.n
        P.on("sync", arb)

        def g_i2(o, w1=v1):
            o.wait_ge(svv.h, w1)
            o.iota(W64[:, 320:384], [[1, 64]], channel_multiplier=0,
                   allow_small_or_imprecise_dtypes=True)
            o.iota(SMF[0:64, 16:17], [[1, 1]], channel_multiplier=1,
                   allow_small_or_imprecise_dtypes=True).then_inc(sgp.h, 1)
        g2 = sgp.inc()
        P.on("gpsimd", g_i2)

        def v_w64(o, g2=g2):
            o.wait_ge(sgp.h, g2)
            AC = S64[:, 256:320]
            o.tensor_single_scalar(AC, W64[:, 320:384], SMF[0:64, 16:17], ALU.mult)
            F = S64[:, 320:384]
            T = S64[:, 384:448]
            o.drain()
            o.tensor_single_scalar(F, AC, 1.0 / 64.0, ALU.mult)
            o.drain()
            for b in (32.0, 16.0, 8.0, 4.0, 2.0, 1.0):
                o.tensor_single_scalar(T, F, b, ALU.is_ge)
                o.drain()
                o.scalar_tensor_tensor(F, T, -b, F, ALU.mult, ALU.add)
                o.drain()
            o.tensor_single_scalar(T, F, 0.5, ALU.is_ge)
            o.drain()
            o.scalar_tensor_tensor(F, T, -1.0, F, ALU.mult, ALU.add)
            FT = S64[:, 448:512]
            o.tensor_single_scalar(FT, AC, 1.0 / 4096.0, ALU.mult)
            o.drain()
            o.tensor_single_scalar(T, FT, 0.5, ALU.is_ge)
            o.drain()
            o.scalar_tensor_tensor(FT, T, -1.0, FT, ALU.mult, ALU.add
                                   ).then_inc(svv.h, 1)
        v7 = svv.inc()
        P.on("vector", v_w64)

        def s_w64(o, v7=v7):
            o.wait_ge(svv.h, v7)
            F, FT = S64[:, 320:384], S64[:, 448:512]
            o.activation(W64[:, 64:128], F, AF.Sin, scale=float(2 * math.pi))
            o.activation(W64[:, 0:64], F, AF.Sin, scale=float(math.pi))
            o.activation(W64[:, 256:320], FT, AF.Sin, scale=float(2 * math.pi))
            o.activation(W64[:, 192:256], FT, AF.Sin, scale=float(math.pi)
                         ).then_inc(ssc.h, 1)
        s2 = ssc.inc()
        P.on("scalar", s_w64)

        def v_w64b(o, s2=s2, t=ar_done):
            o.wait_ge(ssc.h, s2)
            o.tensor_mul(W64[:, 0:64], W64[:, 0:64], W64[:, 0:64])
            o.drain()
            o.tensor_scalar(W64[:, 0:64], W64[:, 0:64], -2.0, 1.0, ALU.mult, ALU.add)
            o.tensor_mul(W64[:, 192:256], W64[:, 192:256], W64[:, 192:256])
            o.drain()
            o.tensor_scalar(W64[:, 192:256], W64[:, 192:256], -2.0, 1.0,
                            ALU.mult, ALU.add)
            o.tensor_scalar(W64[:, 128:192], W64[:, 64:128], -1.0, None, ALU.mult)
            o.wait_ge(sbc.h, t)
            o.tensor_scalar(S64[:, 256:320], S64[:, 192:256], -1.0, None, ALU.mult
                            ).then_inc(svv.h, 1)
        v8 = svv.inc()
        P.on("vector", v_w64b)

        def t_s1(o, v8=v8):
            o.wait_ge(svv.h, v8)
            arRe, arIm, nAr = S64[:, 128:192], S64[:, 192:256], S64[:, 256:320]
            wre, wim = W64[:, 0:64], W64[:, 64:128]
            o.matmul(PS[3][0:64, 0:64], arRe, wre, start=True, stop=False)
            o.matmul(PS[3][0:64, 0:64], nAr, wim, start=False, stop=True)
            o.matmul(PS[4][0:64, 0:64], arRe, wim, start=True, stop=False)
            o.matmul(PS[4][0:64, 0:64], arIm, wre, start=False, stop=True
                     ).then_inc(smm.h, 1)
        m2 = smm.inc()
        P.on("tensor", t_s1)

        def v_tw(o, m2=m2):
            o.wait_ge(smm.h, m2)
            SRE, SIM = S64[:, 320:384], S64[:, 384:448]
            o.tensor_copy(SRE, PS[3][0:64, 0:64])
            o.tensor_copy(SIM, PS[4][0:64, 0:64])
            o.drain()
            twre, twim = W64[:, 192:256], W64[:, 256:320]
            TRE, TIM = S64[:, 448:512], S64[:, 128:192]
            o.tensor_mul(TRE, SRE, twre)
            o.tensor_mul(KSB[:, :], SIM, twim)
            o.drain()
            o.tensor_sub(TRE, TRE, KSB[0:64, :])
            o.tensor_mul(TIM, SRE, twim)
            o.drain()
            o.tensor_mul(KSB[:, :], SIM, twre)
            o.drain()
            o.tensor_add(TIM, TIM, KSB[0:64, :])
            o.drain()
            o.tensor_scalar(S64[:, 192:256], TIM, -1.0, None, ALU.mult
                            ).then_inc(svv.h, 1)
        v9 = svv.inc()
        P.on("vector", v_tw)

        def t_s2(o, v9=v9):
            o.wait_ge(svv.h, v9)
            TRE, nTIM = S64[:, 448:512], S64[:, 192:256]
            o.matmul(PS[5][0:64, 0:64], TRE, W64[:, 0:64], start=True, stop=False)
            o.matmul(PS[5][0:64, 0:64], nTIM, W64[:, 64:128], start=False,
                     stop=True).then_inc(smm.h, 1)
        m3 = smm.inc()
        P.on("tensor", t_s2)

        def s_kev(o, m3=m3):
            o.wait_ge(smm.h, m3)
            o.mul(KSB[:, :], PS[5][0:64, 0:64], float(1.0 / 4096.0)).then_inc(ssc.h, 1)
        s3 = ssc.inc()
        P.on("scalar", s_kev)

        def v_kd(o, s3=s3):
            o.wait_ge(ssc.h, s3)
            o.tensor_add(KSB[0:1, 0:1], KSB[0:1, 0:1], SMF[0:1, 0:1]
                         ).then_inc(svv.h, 1)
        v10 = svv.inc()
        P.on("vector", v_kd)

        def t_ktr(o, v10=v10):
            o.wait_ge(svv.h, v10)
            o.transpose(PS[6][0:64, 0:64], KSB[:, :], CONF[0:64, 0:64]).then_inc(smm.h, 1)
        m4 = smm.inc()
        P.on("tensor", t_ktr)

        def v_ktb(o, m4=m4):
            o.wait_ge(smm.h, m4)
            o.memset(SCB[:, 0:64], 0.0)
            o.tensor_copy(KTB[:, :], PS[6][0:64, 0:64]).then_inc(svv.h, 1)
        v11 = svv.inc()
        P.on("vector", v_ktb)

        KREPSB = PONG[:, 0:8192]

        def kw(o, t=v11, base=sbc.n):
            o.wait_ge(svv.h, t)
            o.dma_start(out=bass.AP(kpad, 0, [[1, 8192]]),
                        in_=SCB[:, 0:64]).then_inc(sbc.h, 16)
            o.wait_ge(sbc.h, base + 16)
            o.dma_start(out=bass.AP(kpad, 4095, [[64, 64], [1, 64]]),
                        in_=KTB[:, :]).then_inc(sbc.h, 16)
            o.wait_ge(sbc.h, base + 32)
            o.dma_start(out=KREPSB, in_=bass.AP(kpad, 0, [[0, 128], [1, 8192]])
                        ).then_inc(sbc.h, 16)
            o.wait_ge(sbc.h, base + 48)
            o.dma_start(out=bass.AP(krep, 0, [[8193, 128], [1, 8192]]),
                        in_=KREPSB).then_inc(sbc.h, 16)
            o.wait_ge(sbc.h, base + 64)
        sbc.n += 64
        P.on("sync", kw)

    def ln(src_f32, dst_ping, is_post=False, y_src=None, half=0, upto="full",
           y_ready=None):
        for c in range(CH):
            if is_post:
                srcb = chunk(y_src, c)
            else:
                srcb = SCB[:, (c % 4) * 512:(c % 4 + 1) * 512]
            sq = SCB[:, (4 + c % 4) * 512:(5 + c % 4) * 512]

            cp_t = None
            if not is_post:
                def vcopy(o, c=c, srcb=srcb, wt=ssu.n - 3):
                    if c >= 4:
                        o.wait_ge(ssu.h, wt)
                    o.tensor_copy(srcb, chunk(src_f32, c)).then_inc(svv.h, 1)
                cp_t = svv.inc()
                P.on("vector", vcopy)

            def sstat(o, c=c, srcb=srcb, sq=sq, wt=ssu.n - 3, cp_t=cp_t,
                      yr=(y_ready[c] if (is_post and y_ready) else None)):
                if yr is not None:
                    o.wait_ge(sev.h, yr)
                if c >= 4:
                    o.wait_ge(ssu.h, wt)
                if cp_t is not None:
                    o.wait_ge(svv.h, cp_t)
                o.activation(sq, srcb, AF.Square).then_inc(ssc.h, 1)
            sq_t = ssc.inc()
            P.on("scalar", sstat)

            def tmm(o, c=c, srcb=srcb, sq=sq, sq_t=sq_t):
                o.wait_ge(ssc.h, sq_t)
                o.matmul(PS[7][:, :], ONES, sq,
                         start=(c == 0), stop=(c == CH - 1))
                o.matmul(PS[6][:, :], ONES, srcb,
                         start=(c == 0), stop=(c == CH - 1)).then_inc(ssu.h, 1)
                if c == CH - 1:
                    o.drain().then_inc(smm.h, 1)
            ssu.inc()
            P.on("tensor", tmm)
        stats_done = smm.inc()
        if upto == "stats":
            P.on("tensor", lambda o, t=stats_done: o.wait_ge(smm.h, t))
            return

        def vrows(o, t=stats_done):
            o.wait_ge(smm.h, t)
            o.tensor_single_scalar(ABC[:, :], PS[6][:, :], 1.0 / 4096.0, ALU.mult)
            o.tensor_single_scalar(BBC[:, :], PS[7][:, :], 1.0 / 4096.0, ALU.mult)
            o.drain()
            o.tensor_mul(SCF[:, 0:512], ABC[:, :], ABC[:, :])
            o.drain()
            o.tensor_sub(BBC[:, :], BBC[:, :], SCF[:, 0:512])
            o.drain()
            o.tensor_single_scalar(BBC[:, :], BBC[:, :], float(EPS), ALU.add
                                   ).then_inc(svv.h, 1)
        var_done = svv.inc()
        P.on("vector", vrows)
        if upto == "vrows":
            P.on("vector", lambda o, t=var_done: o.wait_ge(svv.h, t))
            return

        def srstd(o, t=var_done):
            o.wait_ge(svv.h, t)
            o.activation(SCF[:, 0:512], BBC[:, :], AF.Ln)
            o.drain()
            o.activation(BBC[:, :], SCF[:, 0:512], AF.Exp, scale=-0.5
                         ).then_inc(ssc.h, 1)
        rstd_done = ssc.inc()
        P.on("scalar", srstd)
        if upto == "srstd":
            P.on("scalar", lambda o, t=rstd_done: o.wait_ge(ssc.h, t))
            return

        def vbrow(o, t=rstd_done, post=is_post, half=half):
            o.wait_ge(ssc.h, t)
            last = o.tensor_mul(ABC[:, :], ABC[:, :], BBC[:, :])
            if post:
                o.drain()
                last = o.tensor_add(RDEF[:, half * 512:(half + 1) * 512],
                                    RDEF[:, half * 512:(half + 1) * 512],
                                    ABC[:, :])
            last.then_inc(svv.h, 1)
        brow_done = svv.inc()
        P.on("vector", vbrow)
        if upto == "rows":
            P.on("vector", lambda o, t=brow_done: o.wait_ge(svv.h, t))
            return
        P.on("vector", lambda o: o.drain())

        for c0 in range(0, CH, 2):
            if is_post:
                def vap(o, c0=c0):
                    t0 = SCF[:, (c0 % 4) * 512:(c0 % 4 + 1) * 512]
                    t1 = SCF[:, (c0 % 4 + 1) * 512:(c0 % 4 + 2) * 512]
                    o.tensor_mul(t0, chunk(y_src, c0), BBC[:, :])
                    o.tensor_mul(t1, chunk(y_src, c0 + 1), BBC[:, :])
                    o.drain()
                    o.tensor_add(chunk(src_f32, c0), chunk(src_f32, c0), t0
                                 ).then_inc(svv.h, 1)
                    o.tensor_add(chunk(src_f32, c0 + 1), chunk(src_f32, c0 + 1),
                                 t1).then_inc(svv.h, 1)
            else:
                def vap(o, c0=c0):
                    t0 = SCF[:, (c0 % 4) * 512:(c0 % 4 + 1) * 512]
                    t1 = SCF[:, (c0 % 4 + 1) * 512:(c0 % 4 + 2) * 512]
                    o.tensor_mul(t0, chunk(src_f32, c0), BBC[:, :])
                    o.tensor_mul(t1, chunk(src_f32, c0 + 1), BBC[:, :])
                    o.drain()
                    o.tensor_sub(chunk(dst_ping, c0), t0, ABC[:, :]
                                 ).then_inc(svv.h, 1)
                    o.tensor_sub(chunk(dst_ping, c0 + 1), t1, ABC[:, :]
                                 ).then_inc(svv.h, 1)
            svv.inc(2)
            P.on("vector", vap)
        return svv.n - CH

    def projection(w_dram, m_chunks, rhs_buf, evac, evac_eng="scalar",
                   tri=False, extra=None, rhs_ready=None, evac_guard=None):
        bank_sets = [(0, 1, 2), (3, 4, 5)]
        n_groups = (m_chunks + GW - 1) // GW
        ev_t = {}
        ev_entry = sev.n
        for g in range(n_groups):
            m0 = g * GW
            mw = min(GW, m_chunks - m0)
            banks = bank_sets[g % 2]
            ks = list(range(min(CH, m0 + mw))) if tri else list(range(CH))
            if g >= 2:
                P.on("tensor", lambda o, t=ev_t[g - 2]: o.wait_ge(sev.h, t))
            elif ev_entry > 0:
                P.on("tensor", lambda o, t=ev_entry: o.wait_ge(sev.h, t))
            SLAB = 8
            slabs = [ks[i:i + SLAB] for i in range(0, len(ks), SLAB)]
            ring = TR if tri else WR
            rel = stu if tri else swu
            for si, sl in enumerate(slabs):
                nsl = len(sl)
                if tri:
                    slot = (std.n // 16) % 2
                    ssem = stsl[slot]
                else:
                    slot = (swd.n // 16) % 2
                    ssem = swsl[slot]
                ub = rel.n - 1

                def load(o, sl=sl, nsl=nsl, m0=m0, mw=mw, slot=slot, ub=ub,
                         ssem=ssem, tri=tri, w=w_dram, rel=rel):
                    if ub > 0:
                        o.wait_ge(rel.h, ub)
                    if tri:
                        base = 4095 + m0 * 128 - 128 * sl[-1]
                        o.dma_start(
                            out=TR[:, slot * 3072:slot * 3072 + nsl * mw * 128],
                            in_=bass.AP(krep, base,
                                        [[8192, 128], [128, nsl], [1, mw * 128]])
                        ).then_inc(ssem.h, 16)
                    else:
                        o.dma_start(
                            out=WR[:, slot * 3072:slot * 3072 + nsl * mw * 128],
                            in_=bass.AP(w, sl[0] * 128 * D + m0 * 128,
                                        [[D, 128], [128 * D, nsl], [1, mw * 128]])
                        ).then_inc(ssem.h, 16)
                (std if tri else swd).inc(16)
                dma_t = ssem.inc(16)
                P.on("gpsimd", load)

                def mms(o, sl=sl, nsl=nsl, si=si, m0=m0, mw=mw, slot=slot,
                        banks=banks, dma_t=dma_t, tri=tri, nks=len(ks),
                        has_x=extra is not None, ssem=ssem, rel=rel, ring=ring,
                        g=g, rhs_ready=rhs_ready):
                    if rhs_ready is not None and (g == 0 or tri):
                        o.wait_ge(svv.h, rhs_ready + sl[-1] + 1)
                    o.wait_ge(ssem.h, dma_t)
                    last = None
                    for ki, k in enumerate(sl):
                        j = (sl[-1] - k) if tri else ki
                        tbase = slot * 3072 + j * mw * 128
                        for mi in range(mw):
                            m = m0 + mi
                            if tri and k > m:
                                continue
                            st = (k == 0)
                            sp = ((k == m) if tri else
                                  (k == nks - 1 and not has_x))
                            last = o.matmul(
                                PS[banks[mi]][:, :],
                                ring[:, tbase + mi * 128:tbase + (mi + 1) * 128],
                                rhs_buf[:, k * TH:(k + 1) * TH],
                                start=st, stop=sp)
                    last.then_inc(rel.h, 1)
                rel.inc()
                P.on("tensor", mms)
            if extra is not None:
                xdram, xparts = extra
                nxk = len(xparts)
                xoff = 0
                for xk, (rh, xrhs) in enumerate(xparts):
                    slot = (std.n // 16) % 2
                    ub = stu.n - 1
                    ssem = sxsl[slot]

                    def loadx(o, xoff=xoff, rh=rh, m0=m0, mw=mw, slot=slot,
                              ub=ub, xdram=xdram, ssem=ssem):
                        if ub > 0:
                            o.wait_ge(stu.h, ub)
                        o.dma_start(
                            out=TR[0:rh, 6144 + (slot % 2) * 512:
                                   6144 + (slot % 2) * 512 + mw * 128],
                            in_=xdram[xoff:xoff + rh,
                                      m0 * 128:m0 * 128 + mw * 128]
                        ).then_inc(ssem.h, 16)
                    std.inc(16)
                    dma_t = ssem.inc(16)
                    P.on("gpsimd", loadx)

                    def mmx(o, xk=xk, rh=rh, m0=m0, mw=mw, slot=slot,
                            banks=banks, dma_t=dma_t, xrhs=xrhs, nxk=nxk,
                            ssem=ssem):
                        o.wait_ge(ssem.h, dma_t)
                        last = None
                        for mi in range(mw):
                            tb = 6144 + (slot % 2) * 512 + mi * 128
                            last = o.matmul(PS[banks[mi]][:, :],
                                     TR[0:rh, tb:tb + 128],
                                     xrhs, start=False, stop=(xk == nxk - 1))
                        last.then_inc(stu.h, 1)
                    stu.inc()
                    P.on("tensor", mmx)
                    xoff += rh
            grp_done = smm.inc()
            P.on("tensor", lambda o: o.drain().then_inc(smm.h, 1))

            def ev(o, m0=m0, mw=mw, banks=banks, t=grp_done, g=g,
                   evac_guard=evac_guard):
                o.wait_ge(smm.h, t)
                if g == 0 and evac_guard is not None:
                    o.wait_ge(svv.h, evac_guard)
                last = None
                for mi in range(mw):
                    last = evac(o, m0 + mi, PS[banks[mi]][:, :])
                last.then_inc(sev.h, 1)
            ev_t[g] = sev.inc()
            P.on(evac_eng, ev)
        return {c: ev_t[c // GW] for c in range(m_chunks)}

    def thin_proj(w_dram, width, psum_ap, rhs_buf):
        rs = w_dram.shape[1]
        for k0 in range(0, CH, 8):
            nsl = min(8, CH - k0)
            slot = (swd.n // 16) % 2
            ub = swu.n - 1
            ssem = swsl[slot]

            def load(o, k0=k0, nsl=nsl, slot=slot, ub=ub, w=w_dram,
                     width=width, ssem=ssem, rs=rs):
                if ub > 0:
                    o.wait_ge(swu.h, ub)
                o.dma_start(out=WR[:, slot * 3072:slot * 3072 + nsl * width],
                            in_=bass.AP(w, k0 * 128 * rs,
                                        [[rs, 128], [128 * rs, nsl], [1, width]])
                            ).then_inc(ssem.h, 16)
            swd.inc(16)
            dma_t = ssem.inc(16)
            P.on("gpsimd", load)

            def mm(o, k0=k0, nsl=nsl, slot=slot, dma_t=dma_t, width=width,
                   ps=psum_ap, ssem=ssem):
                o.wait_ge(ssem.h, dma_t)
                last = None
                for j in range(nsl):
                    k = k0 + j
                    last = o.matmul(
                        ps, WR[:, slot * 3072 + j * width:slot * 3072 + (j + 1) * width],
                        rhs_buf[:, k * TH:(k + 1) * TH],
                        start=(k == 0), stop=(k == CH - 1))
                last.then_inc(swu.h, 1)
                if k0 + nsl == CH:
                    o.drain().then_inc(smm.h, 1)
            swu.inc()
            P.on("tensor", mm)
        return smm.inc()

    def load_xh(o_src_fn, pre_waits):
        def go(o):
            for s, v in pre_waits:
                o.wait_ge(s, v)
            o_src_fn(o)
        t = sxd.inc(16)
        P.on("sync", go)
        for e in ("vector", "scalar", "tensor"):
            P.on(e, lambda o, t=t: o.wait_ge(sxd.h, t))

    def evac_pong(o, m, psum):
        return o.copy(chunk(PONG, m), psum)

    def a_phase(h):
        pre = [(sxd.h, sxd.n), (sod.h, sod.n)]
        load_xh(lambda o, h=h: o.dma_start(
            out=XH[:, :],
            in_=bass.AP(xin, h * TH, [[NTOK, 128], [NTOK * 128, CH], [1, TH]])
        ).then_inc(sxd.h, 16), pre)
        P.on("scalar", lambda o, t=sqd.n: o.wait_ge(sqd.h, t))
        barrier()
        if STAGE == 1 and SUB == "pb":
            ln(XH, PING, half=h)
            barrier()
            projection(None, CH, PING, evac_pong, tri=True)
            barrier()
            for c in range(CH):
                def vd(o, c=c):
                    o.tensor_copy(chunk(XH, c), chunk(PONG, c)).then_inc(svv.h, 1)
                svv.inc()
                P.on("vector", vd)
            barrier()
            return
        if STAGE == 1 and SUB == "pa":
            ln(XH, PING, half=h)
            barrier()
            for c in range(CH):
                def vd(o, c=c):
                    o.tensor_copy(chunk(XH, c), chunk(PING, c)).then_inc(svv.h, 1)
                svv.inc()
                P.on("vector", vd)
            barrier()
            return
        if STAGE == 1 and SUB in ("s1", "s2", "s3", "v", "r"):
            ln(XH, PING, half=h,
               upto={"s1": "stats", "s2": "rows", "s3": "bcast",
                     "v": "vrows", "r": "srstd"}[SUB])
            barrier()
            return
        ap = ln(XH, PING, half=h)
        ev = projection(None, CH, PING, evac_pong, tri=True, rhs_ready=ap)
        ln(XH, None, is_post=True, y_src=PONG, half=h, y_ready=ev)
        if STAGE < 2:
            barrier()
            return
        ln(XH, PING, half=h)
        barrier()
        l0_ffn(h)
        barrier()
        ln(XH, None, is_post=True, y_src=PONG, half=h)
        barrier()
        if STAGE < 3:
            return
        ln(XH, PING, half=h)
        barrier()
        kvq(h)
        def spx(o, h=h):
            o.dma_start(out=bass.AP(xspill, h * D * TH, [[CH * TH, 128], [1, CH * TH]]),
                        in_=XH[:, :]).then_inc(sxd.h, 16)
        sxd.inc(16)
        P.on("scalar", spx)
        barrier()

    def l0_ffn(h):
        lg = thin_proj(l0aux, 128, PS[6][:, :], PING)

        def s_lg(o, t=lg):
            o.wait_ge(smm.h, t)
            o.activation(SCF[32:36, 0:512], PS[6][32:36, 0:512], AF.Identity,
                         bias=SMF[32:36, 40:41]).then_inc(ssc.h, 1)
        lge = ssc.inc()
        P.on("scalar", s_lg)
        barrier()
        for t4 in range(4):
            def t_tp(o, t4=t4, wv=svv.n - 1):
                if t4 >= 2:
                    o.wait_ge(svv.h, wv)
                o.transpose(PS[t4 % 2][0:128, 0:4],
                            SCF[32:36, t4 * 128:(t4 + 1) * 128],
                            CONF[32:36, 32:36]).then_inc(smm.h, 1)
            mt = smm.inc()
            P.on("tensor", t_tp)

            def v_m(o, t4=t4, mt=mt):
                o.wait_ge(smm.h, mt)
                lt = PS[t4 % 2][0:128, 0:4]
                o.reduce_max(SMF[:, 8:9], lt, axis=AX.X)
                o.drain()
                o.tensor_single_scalar(SCF[:, 512 + t4 * 4:512 + (t4 + 1) * 4],
                                       lt, SMF[:, 8:9], ALU.is_equal
                                       ).then_inc(svv.h, 1)
            svv.inc()
            P.on("vector", v_m)
        barrier()
        for t4 in range(4):
            def t_tb(o, t4=t4, wv=svv.n - 1):
                if t4 >= 2:
                    o.wait_ge(svv.h, wv)
                o.transpose(PS[t4 % 2][0:4, 0:128],
                            SCF[:, 512 + t4 * 4:512 + (t4 + 1) * 4],
                            CONF[0:128, 0:128]).then_inc(smm.h, 1)
            mt = smm.inc()
            P.on("tensor", t_tb)

            def v_me(o, t4=t4, mt=mt):
                o.wait_ge(smm.h, mt)
                o.tensor_copy(SCF[0:4, 1024 + t4 * 128:1024 + (t4 + 1) * 128],
                              PS[t4 % 2][0:4, 0:128]).then_inc(svv.h, 1)
            svv.inc()
            P.on("vector", v_me)
        barrier()
        def t_e4(o):
            o.matmul(PS[7][0:32, 0:512], C_E4, SCF[0:4, 1024:1536],
                     start=True, stop=True).then_inc(smm.h, 1)
        m32 = smm.inc()
        P.on("tensor", t_e4)

        def v_zh(o, m32=m32):
            o.wait_ge(smm.h, m32)
            o.tensor_copy(SCF[0:32, 512:1024], PS[7][0:32, 0:512])
            o.drain()
            o.tensor_mul(GT[0:32, 0:512], PS[6][0:32, 0:512], SCF[0:32, 512:1024]
                         ).then_inc(svv.h, 1)
        svv.inc()
        P.on("vector", v_zh)
        barrier()
        projection(wupT, CH, PING, evac_pong,
                   extra=(lb0T, [(128, GT[:, 0:512])]))

    def kvq(h):
        kt = thin_proj(wk, 128, PS[6][:, :], PING)

        def s_kev(o, t=kt):
            o.wait_ge(smm.h, t)
            o.activation(SCB[:, 0:512], PS[6][:, :], AF.Identity,
                         bias=SMF[:, 32:33]).then_inc(ssc.h, 1)
        ke = ssc.inc()
        P.on("scalar", s_kev)

        def d_k(o, h=h, ke=ke):
            o.wait_ge(ssc.h, ke)
            o.dma_start(out=bass.AP(ccin, h * 512, [[NTOK, 128], [1, 512]]),
                        in_=SCB[:, 0:512]).then_inc(sxd.h, 16)
        sxd.inc(16)
        P.on("sync", d_k)
        barrier()
        vt = thin_proj(wv, 128, PS[6][:, :], PING)

        def s_vev(o, t=vt):
            o.wait_ge(smm.h, t)
            o.activation(SCF[:, 0:512], PS[6][:, :], AF.Identity,
                         bias=SMF[:, 33:34]).then_inc(ssc.h, 1)
        ve = ssc.inc()
        P.on("scalar", s_vev)
        for t4 in range(4):
            def t_vt(o, t4=t4, ve=ve, wv=svv.n - 1):
                o.wait_ge(ssc.h, ve)
                if t4 >= 2:
                    o.wait_ge(svv.h, wv)
                o.transpose(PS[t4 % 2][0:128, 0:128],
                            SCF[:, t4 * 128:(t4 + 1) * 128],
                            CONF[0:128, 0:128]).then_inc(smm.h, 1)
            mt = smm.inc()
            P.on("tensor", t_vt)

            def v_vc(o, t4=t4, mt=mt):
                o.wait_ge(smm.h, mt)
                o.tensor_copy(SCB[:, 512 + t4 * 128:512 + (t4 + 1) * 128],
                              PS[t4 % 2][0:128, 0:128]).then_inc(svv.h, 1)
            svv.inc()
            P.on("vector", v_vc)
        vtc = svv.n

        def d_v(o, h=h, vtc=vtc):
            o.wait_ge(svv.h, vtc)
            o.dma_start(out=bass.AP(ccin, 128 * NTOK + h * 512 * 128,
                                    [[128, 128], [16384, 4], [1, 128]]),
                        in_=SCB[:, 512:1024]).then_inc(sxd.h, 16)
        sxd.inc(16)
        P.on("sync", d_v)
        barrier()

        def evac_q(o, m, psum):
            return o.activation(chunk(PONG, m), psum, AF.Identity,
                                bias=CONF[:, 128 + m:129 + m])
        projection(wq, CH, PING, evac_q)
        barrier()
        for m in range(CH):
            def t_qn(o, m=m, wv=svv.n - 1):
                if m >= 2:
                    o.wait_ge(svv.h, wv)
                o.matmul(PS[6 + m % 2][:, :], ONES, chunk(PONG, m),
                         start=True, stop=True).then_inc(smm.h, 1)
            mt = smm.inc()
            P.on("tensor", t_qn)

            def v_qn(o, m=m, h=h, mt=mt):
                o.wait_ge(smm.h, mt)
                o.reduce_max(QN[:, h * 32 + m:h * 32 + m + 1],
                             PS[6 + m % 2][:, :], axis=AX.X).then_inc(svv.h, 1)
            svv.inc()
            P.on("vector", v_qn)

        def d_q(o, h=h, wv=svv.n):
            o.wait_ge(svv.h, wv)
            o.dma_start(out=bass.AP(qspill, h * 512,
                                    [[NTOK, 128], [NTOK * 128, CH], [1, 512]]),
                        in_=PONG[:, :]).then_inc(sqd.h, 16)
        sqd.inc(16)
        P.on("sync", d_q)
        barrier()

    def exchange_and_prep():
        def g_cc(o, t=sxd.n):
            o.wait_ge(sxd.h, t)
            o.collective_compute(
                "AllGather", ALU.bypass,
                replica_groups=[[0, 1], [2, 3], [4, 5], [6, 7]],
                ins=[ccin.ap().opt()],
                outs=[ccout.ap().opt()],
            ).then_inc(scc.h, 1)
            o.wait_ge(scc.h, 1)
        scc.inc()
        P.on("gpsimd", g_cc)
        barrier()

        def d_kv(o, t=scc.n):
            o.wait_ge(scc.h, t)
            for r in range(2):
                o.dma_start(out=KFM[:, r * 1024:(r + 1) * 1024],
                            in_=bass.AP(ccout, r * 262144,
                                        [[1024, 128], [1, 1024]])).then_inc(sxd.h, 16)
                o.dma_start(out=VTM[:, r * 1024:(r + 1) * 1024],
                            in_=bass.AP(ccout, r * 262144 + 131072,
                                        [[128, 128], [16384, 8], [1, 128]])
                            ).then_inc(sxd.h, 16)
        t = sxd.inc(64)
        P.on("sync", d_kv)
        for e in ("vector", "scalar", "tensor"):
            P.on(e, lambda o, t=t: o.wait_ge(sxd.h, t))
        for t4 in range(4):
            def s_k2(o, t4=t4):
                o.activation(SCB[:, 0:512], KFM[:, t4 * 512:(t4 + 1) * 512],
                             AF.Square).then_inc(ssc.h, 1)
            st = ssc.inc()
            P.on("scalar", s_k2)

            def t_k2(o, st=st):
                o.wait_ge(ssc.h, st)
                o.matmul(PS[6][:, :], ONES, SCB[:, 0:512],
                         start=True, stop=True).then_inc(smm.h, 1)
            mt = smm.inc()
            P.on("tensor", t_k2)

            def v_k2(o, t4=t4, mt=mt):
                o.wait_ge(smm.h, mt)
                o.reduce_max(QN[:, 64 + t4:65 + t4], PS[6][:, :], axis=AX.X
                             ).then_inc(svv.h, 1)
            svv.inc()
            P.on("vector", v_k2)
            barrier()

        def v_bound(o):
            o.reduce_max(QN[:, 68:69], QN[:, 64:68], axis=AX.X)
            o.tensor_max(NEGB[:, :], QN[:, 0:32], QN[:, 32:64])
            o.drain()
            o.tensor_single_scalar(NEGB[:, :], NEGB[:, :], QN[:, 68:69],
                                   ALU.mult)
            o.drain()
            o.tensor_single_scalar(NEGB[:, :], NEGB[:, :], 1e-12, ALU.add
                                   ).then_inc(svv.h, 1)
        vb = svv.inc()
        P.on("vector", v_bound)

        def s_bound(o, vb=vb):
            o.wait_ge(svv.h, vb)
            o.activation(QN[:, 64:96], NEGB[:, :], AF.Ln)
            o.drain()
            o.activation(NEGB[:, :], QN[:, 64:96], AF.Exp, scale=0.5
                         ).then_inc(ssc.h, 1)
        sbd = ssc.inc()
        P.on("scalar", s_bound)

        def v_bound2(o, t=sbd):
            o.wait_ge(ssc.h, t)
            o.tensor_scalar(NEGB[:, :], NEGB[:, :],
                            float(-1.05 / math.sqrt(128.0)), -0.5,
                            ALU.mult, ALU.add).then_inc(svv.h, 1)
        svv.inc()
        P.on("vector", v_bound2)
        barrier()

    def b_phase(h):
        def d_qh(o, hd, h=h):
            o.dma_start(out=QHB[:, (hd % 2) * 512:(hd % 2 + 1) * 512],
                        in_=bass.AP(qspill, hd * DH * NTOK + h * 512,
                                    [[NTOK, 128], [1, 512]])
                        ).then_inc(sqsl[hd % 2].h, 16)

        P.on("sync", lambda o: d_qh(o, 0))
        q_t = {0: sqsl[0].inc(16)}
        qk_last = {}
        vr_t = {}
        vatt_t = {}
        exp_prev = None

        for hd in range(NH):
            if hd + 1 < NH:
                def pf(o, n=hd + 1, w=qk_last.get(hd - 1)):
                    if w is not None:
                        o.wait_ge(smm.h, w)
                    d_qh(o, n)
                P.on("sync", pf)
                q_t[hd + 1] = sqsl[(hd + 1) % 2].inc(16)
            exp_t = {}
            av_t = {}
            for kc in range(16):
                def t_qk(o, kc=kc, hd=hd, qt=q_t[hd], exp_t=exp_t,
                         exp_prev=exp_prev):
                    if kc == 0:
                        o.wait_ge(sqsl[hd % 2].h, qt)
                    if kc >= 2:
                        o.wait_ge(ssc.h, exp_t[kc - 2])
                    elif exp_prev is not None:
                        o.wait_ge(ssc.h, exp_prev[14 + kc])
                    o.matmul(PS[kc % 2][:, :], KFM[:, kc * 128:(kc + 1) * 128],
                             QHB[:, (hd % 2) * 512:(hd % 2 + 1) * 512],
                             start=True, stop=True).then_inc(smm.h, 1)
                qk_t = smm.inc()
                if kc == 15:
                    qk_last[hd] = qk_t
                P.on("tensor", t_qk)

                def s_exp(o, kc=kc, hd=hd, qk_t=qk_t, av_t=av_t):
                    o.wait_ge(smm.h, qk_t)
                    if kc >= 4:
                        o.wait_ge(smm.h, av_t[kc - 4])
                    o.activation(PT[:, (kc % 4) * 512:(kc % 4 + 1) * 512],
                                 PS[kc % 2][:, :], AF.Exp,
                                 bias=NEGB[:, hd:hd + 1],
                                 scale=float(1.0 / math.sqrt(128.0))
                                 ).then_inc(ssc.h, 1)
                exp_t[kc] = ssc.inc()
                P.on("scalar", s_exp)

                def t_av(o, kc=kc, hd=hd, et=exp_t[kc],
                         w_vr=vr_t.get(hd - 1), w_va=vatt_t.get(hd - 2)):
                    o.wait_ge(ssc.h, et)
                    if kc == 0:
                        if w_vr is not None:
                            o.wait_ge(svv.h, w_vr)
                        if w_va is not None:
                            o.wait_ge(svv.h, w_va)
                    pt = PT[:, (kc % 4) * 512:(kc % 4 + 1) * 512]
                    o.matmul(PS[4][:, :], ONES, pt,
                             start=(kc == 0), stop=(kc == 15))
                    o.matmul(PS[2 + hd % 2][:, :], VTM[:, kc * 128:(kc + 1) * 128],
                             pt, start=(kc == 0), stop=(kc == 15)
                             ).then_inc(smm.h, 1)
                av_t[kc] = smm.inc()
                P.on("tensor", t_av)
            av_done = smm.n

            def v_r(o, t=av_done):
                o.wait_ge(smm.h, t)
                o.reciprocal(ABC[:, :], PS[4][:, :]).then_inc(svv.h, 1)
            vr_t[hd] = svv.inc()
            P.on("vector", v_r)

            def v_att(o, hd=hd, vr=vr_t[hd]):
                o.wait_ge(svv.h, vr)
                o.drain()
                o.tensor_mul(chunk(PING, hd), PS[2 + hd % 2][:, :], ABC[:, :]
                             ).then_inc(svv.h, 1)
            vatt_t[hd] = svv.inc()
            P.on("vector", v_att)
            exp_prev = exp_t
        barrier()

    def c_phase(h):
        pre = [(sxd.h, sxd.n), (sod.h, sod.n)]
        load_xh(lambda o, h=h: o.dma_start(
            out=XH[:, :],
            in_=bass.AP(xspill, h * D * TH, [[CH * TH, 128], [1, CH * TH]])
        ).then_inc(sxd.h, 16), pre)
        barrier()

        def evac_wo(o, m, psum):
            return o.activation(chunk(PONG, m), psum, AF.Identity,
                                bias=CONF[:, 160 + m:161 + m])
        ev = projection(wo, CH, PING, evac_wo)
        ln(XH, None, is_post=True, y_src=PONG, half=h, y_ready=ev)
        if STAGE >= 4:
            ln(XH, PING, half=h)
            barrier()
            ev = l1_ffn(h)
            ln(XH, None, is_post=True, y_src=PONG, half=h, y_ready=ev)
            barrier()
        else:
            barrier()
        finish_half(h)

    def l1_ffn(h):
        lg = thin_proj(g1T, 128, PS[6][:, :], PING)

        def v_lg(o, t=lg):
            o.wait_ge(smm.h, t)
            o.tensor_copy(SCF[0:28, 0:512], PS[6][0:28, 0:512]).then_inc(svv.h, 1)
        svv.inc()
        P.on("vector", v_lg)
        barrier()
        for t4 in range(4):
            def t_tp(o, t4=t4, wv=svv.n - 1):
                if t4 >= 2:
                    o.wait_ge(svv.h, wv)
                o.transpose(PS[t4 % 2][0:128, 0:28],
                            SCF[0:28, t4 * 128:(t4 + 1) * 128],
                            CONF[0:28, 0:28]).then_inc(smm.h, 1)
            mt = smm.inc()
            P.on("tensor", t_tp)

            def v_t2(o, t4=t4, mt=mt):
                o.wait_ge(smm.h, mt)
                lt = PS[t4 % 2][0:128, 0:28]
                m1 = SCF[:, 512 + t4 * 28:512 + (t4 + 1) * 28]
                m2 = SCF[:, 624 + t4 * 28:624 + (t4 + 1) * 28]
                o.reduce_max(SMF[:, 8:9], lt, axis=AX.X)
                o.drain()
                o.tensor_single_scalar(m1, lt, SMF[:, 8:9], ALU.is_equal)
                o.drain()
                o.scalar_tensor_tensor(m2, m1, -1e30, lt, ALU.mult, ALU.add)
                o.drain()
                o.reduce_max(SMF[:, 12 + t4:13 + t4], m2, axis=AX.X)
                o.drain()
                o.tensor_single_scalar(m2, m2, SMF[:, 12 + t4:13 + t4],
                                       ALU.is_equal)
                o.tensor_sub(SMF[:, 16 + t4:17 + t4],
                             SMF[:, 12 + t4:13 + t4], SMF[:, 8:9]
                             ).then_inc(svv.h, 1)
            svv.inc()
            P.on("vector", v_t2)
        dv = svv.n

        def s_e2(o, dv=dv):
            o.wait_ge(svv.h, dv)
            o.activation(SMF[:, 20:24], SMF[:, 16:20], AF.Exp).then_inc(ssc.h, 1)
        se = ssc.inc()
        P.on("scalar", s_e2)

        def v_p(o, se=se):
            o.wait_ge(ssc.h, se)
            o.tensor_scalar(SMF[:, 24:28], SMF[:, 20:24], 1.0, None, ALU.add)
            o.drain()
            o.reciprocal(SMF[:, 24:28], SMF[:, 24:28])
            o.drain()
            o.tensor_mul(SMF[:, 28:32], SMF[:, 20:24], SMF[:, 24:28])
            o.drain()
            last = None
            for t4 in range(4):
                m1 = SCF[:, 512 + t4 * 28:512 + (t4 + 1) * 28]
                m2 = SCF[:, 624 + t4 * 28:624 + (t4 + 1) * 28]
                o.tensor_single_scalar(m1, m1, SMF[:, 24 + t4:25 + t4], ALU.mult)
                o.tensor_single_scalar(m2, m2, SMF[:, 28 + t4:29 + t4], ALU.mult)
                o.drain()
                last = o.tensor_add(m1, m1, m2)
                o.drain()
            last.then_inc(svv.h, 1)
        svv.inc()
        P.on("vector", v_p)
        barrier()
        for t4 in range(4):
            def t_gb(o, t4=t4, wv=svv.n - 1):
                if t4 >= 2:
                    o.wait_ge(svv.h, wv)
                o.transpose(PS[t4 % 2][0:28, 0:128],
                            SCF[:, 512 + t4 * 28:512 + (t4 + 1) * 28],
                            CONF[0:128, 0:128]).then_inc(smm.h, 1)
            mt = smm.inc()
            P.on("tensor", t_gb)

            def v_ge(o, t4=t4, mt=mt):
                o.wait_ge(smm.h, mt)
                o.tensor_copy(SCF[0:28, 1024 + t4 * 128:1024 + (t4 + 1) * 128],
                              PS[t4 % 2][0:28, 0:128]).then_inc(svv.h, 1)
            svv.inc()
            P.on("vector", v_ge)
        barrier()

        def t_e28(o):
            o.matmul(PS[6][:, :], C_E28A, SCF[0:28, 1024:1536],
                     start=True, stop=True)
            o.matmul(PS[7][:, :], C_E28B, SCF[0:28, 1024:1536],
                     start=True, stop=True).then_inc(smm.h, 1)
        me = smm.inc()
        P.on("tensor", t_e28)

        def v_g224(o, me=me):
            o.wait_ge(smm.h, me)
            o.tensor_copy(GT[:, 0:512], PS[6][:, :])
            o.tensor_copy(GT[:, 512:1024], PS[7][:, :]).then_inc(svv.h, 1)
        svv.inc()
        P.on("vector", v_g224)
        barrier()
        for k0 in range(0, CH, 8):
            nsl = min(8, CH - k0)
            slot = (swd.n // 16) % 2
            ub = swu.n - 1
            ssem = swsl[slot]

            def load(o, k0=k0, nsl=nsl, slot=slot, ub=ub, ssem=ssem):
                if ub > 0:
                    o.wait_ge(swu.h, ub)
                o.dma_start(out=WR[:, slot * 3072:slot * 3072 + nsl * 256],
                            in_=bass.AP(la1T, k0 * 128 * 256,
                                        [[256, 128], [128 * 256, nsl], [1, 256]])
                            ).then_inc(ssem.h, 16)
            swd.inc(16)
            dma_t = ssem.inc(16)
            P.on("gpsimd", load)

            def mm(o, k0=k0, nsl=nsl, slot=slot, dma_t=dma_t, ssem=ssem):
                o.wait_ge(ssem.h, dma_t)
                last = None
                for j in range(nsl):
                    k = k0 + j
                    tb = slot * 3072 + j * 256
                    o.matmul(PS[0][:, :], WR[:, tb:tb + 128],
                             PING[:, k * TH:(k + 1) * TH],
                             start=(k == 0), stop=(k == CH - 1))
                    last = o.matmul(PS[1][:, :], WR[:, tb + 128:tb + 256],
                                    PING[:, k * TH:(k + 1) * TH],
                                    start=(k == 0), stop=(k == CH - 1))
                last.then_inc(swu.h, 1)
                if k0 + nsl == CH:
                    o.drain().then_inc(smm.h, 1)
            swu.inc()
            P.on("tensor", mm)
        ht = smm.inc()

        def v_h224(o, t=ht):
            o.wait_ge(smm.h, t)
            o.tensor_mul(H224[:, 0:512], PS[0][:, :], GT[:, 0:512])
            o.tensor_mul(H224[:, 512:1024], PS[1][:, :],
                         GT[:, 512:1024]).then_inc(svv.h, 1)
        svv.inc()
        P.on("vector", v_h224)
        barrier()
        return projection(bwT, CH, PING, evac_pong,
                           extra=(lb1T, [(128, H224[:, 0:512]),
                                         (128, H224[:, 512:1024])]))

    def finish_half(h):
        for c in range(CH):
            def vf(o, c=c, h=h):
                o.tensor_sub(chunk(XH, c), chunk(XH, c),
                             RDEF[:, h * 512:(h + 1) * 512]).then_inc(svv.h, 1)
            svv.inc()
            P.on("vector", vf)
        fin = svv.n

        def wo_(o, h=h, fin=fin):
            o.wait_ge(svv.h, fin)
            o.dma_start(out=bass.AP(out, h * TH,
                                    [[NTOK, 128], [NTOK * 128, CH], [1, TH]]),
                        in_=XH[:, :]).then_inc(sod.h, 16)
        sod.inc(16)
        P.on("sync", wo_)
        barrier()

    prologue()
    P.on("gpsimd", lambda o, t=sbc.n: o.wait_ge(sbc.h, t))
    barrier()
    if STAGE == 9:
        def vdump(o):
            o.memset(XH[:, :], 0.0)
            o.tensor_copy(XH[:, 0:32], PINGF[:, 15 * 32:16 * 32])
            o.tensor_copy(XH[:, 32:64], PINGF[:, 16 * 32:17 * 32])
            o.tensor_copy(XH[:, 64:96], OMB[:, 192:224])
            o.tensor_copy(XH[:, 96:128], OMB[:, 224:256])
            o.tensor_copy(XH[0:64, 128:192], KSB[:, :])
            o.tensor_copy(XH[:, 192:704], ARL[:, :])
            o.tensor_copy(XH[:, 704:736], OMB[:, 96:128])
            o.tensor_copy(XH[:, 736:768], OMB[:, 128:160])
            o.tensor_copy(XH[:, 768:769], SMF[:, 4:5])
            o.tensor_copy(XH[:, 800:832], OMB[:, 32:64])
            o.tensor_copy(XH[:, 832:864], OMB[:, 160:192])
            o.tensor_copy(XH[:, 864:896], OMB[:, 64:96])
            o.tensor_copy(XH[:, 896:928], PONGF[:, 4096:4128])
            o.tensor_copy(XH[:, 928:960], PONGF[:, 4128:4160])
            o.tensor_copy(XH[:, 960:992], PONGF[:, 4160:4192])
            o.tensor_copy(XH[:, 992:1024], PONGF[:, 4192:4224])
            o.nop().then_inc(svv.h, 1)
        svv.inc()
        P.on("vector", vdump)
        barrier()
        finish_half(0)
        finish_half(1)
    elif STAGE == 0:
        for h in range(2):
            pre = [(sxd.h, sxd.n), (sod.h, sod.n)]
            load_xh(lambda o, h=h: o.dma_start(
                out=XH[:, :],
                in_=bass.AP(xin, h * TH, [[NTOK, 128], [NTOK * 128, CH], [1, TH]])
            ).then_inc(sxd.h, 16), pre)
            barrier()
            finish_half(h)
    elif STAGE < 3:
        a_phase(0)
        finish_half(0)
        a_phase(1)
        finish_half(1)
    else:
        a_phase(0)
        a_phase(1)
        exchange_and_prep()
        b_phase(0)
        c_phase(0)
        b_phase(1)
        c_phase(1)

    P.on("sync", lambda o, t=sod.n: o.wait_ge(sod.h, t))
    P.flush()

    blk.__exit__(None, None, None)
    for g in reversed(guards):
        g.__exit__(None, None, None)
    return nc


def kernel(**inputs):
    f32 = np.float32

    def T(a):
        return np.ascontiguousarray(np.asarray(a, dtype=f32).T)

    u = np.asarray(inputs["L0_attn_u"], dtype=f32).reshape(8192, 4096)
    xinT = np.ascontiguousarray(u.T)

    com = {}
    com["wupT"] = T(inputs["L0_ffn_W_up"])
    l0aux = np.zeros((4096, 128), dtype=f32)
    l0aux[:, 0:32] = np.asarray(inputs["L0_ffn_lora_A"], dtype=f32).reshape(32, 4096).T
    l0aux[:, 32:36] = np.asarray(inputs["L0_ffn_gate_w"], dtype=f32).T
    com["l0aux"] = l0aux
    com["exb0"] = np.asarray(inputs["L0_ffn_expert_bias"], dtype=f32)
    lb0 = np.zeros((128, 4096), dtype=f32)
    lb0[0:32] = np.asarray(inputs["L0_ffn_lora_B"], dtype=f32).transpose(0, 2, 1).reshape(32, 4096)
    com["lb0T"] = lb0
    com["wq"] = np.ascontiguousarray(np.asarray(inputs["L1_attn_Wq"], dtype=f32))
    com["bqT"] = np.ascontiguousarray(
        np.asarray(inputs["L1_attn_bq"], dtype=f32).reshape(32, 128).T)
    com["wk"] = np.ascontiguousarray(np.asarray(inputs["L1_attn_Wk"], dtype=f32))
    com["bk"] = np.asarray(inputs["L1_attn_bk"], dtype=f32)
    com["wv"] = np.ascontiguousarray(np.asarray(inputs["L1_attn_Wv"], dtype=f32))
    com["bv"] = np.asarray(inputs["L1_attn_bv"], dtype=f32)
    com["wo"] = np.ascontiguousarray(np.asarray(inputs["L1_attn_Wo"], dtype=f32))
    com["boT"] = np.ascontiguousarray(
        np.asarray(inputs["L1_attn_bo"], dtype=f32).reshape(32, 128).T)
    com["bwT"] = T(inputs["L1_ffn_base_W"])
    g1p = np.zeros((4096, 128), dtype=f32)
    g1p[:, 0:28] = np.asarray(inputs["L1_ffn_gate_W"], dtype=f32).T
    com["g1T"] = g1p
    la1p = np.zeros((4096, 256), dtype=f32)
    la1p[:, 0:224] = np.asarray(inputs["L1_ffn_lora_A"], dtype=f32).reshape(224, 4096).T
    com["la1T"] = la1p
    lb1p = np.zeros((256, 4096), dtype=f32)
    lb1p[0:224] = np.asarray(inputs["L1_ffn_lora_B"], dtype=f32).transpose(0, 2, 1).reshape(224, 4096)
    com["lb1T"] = lb1p
    com["lam"] = np.asarray(inputs["L0_attn_Lambda_ri"], dtype=f32)
    com["pri"] = np.asarray(inputs["L0_attn_p_ri"], dtype=f32)
    com["qri"] = np.asarray(inputs["L0_attn_q_ri"], dtype=f32)
    vc = np.asarray(inputs["L0_attn_Vc_ri"], dtype=f32)
    com["vcreT"] = np.ascontiguousarray(vc[:, :, 0].T)
    com["vcimT"] = np.ascontiguousarray(vc[:, :, 1].T)
    com["bp"] = np.asarray(inputs["L0_attn_Bp"], dtype=f32)
    com["ctri"] = np.asarray(inputs["L0_attn_Ct_ri"], dtype=f32).reshape(64, 2)
    com["dD"] = np.asarray(inputs["L0_attn_D"], dtype=f32)
    com["logstep"] = np.asarray(inputs["L0_attn_log_step"], dtype=f32)
    com["identf"] = np.eye(128, dtype=f32)
    com["onesb"] = np.ones((128, 128), dtype=ml_dtypes.bfloat16)
    e4 = np.zeros((4, 32), dtype=f32)
    for e in range(4):
        e4[e, e * 8:(e + 1) * 8] = 1.0
    com["exp4"] = e4
    e28 = np.zeros((28, 256), dtype=f32)
    for e in range(28):
        e28[e, e * 8:(e + 1) * 8] = 2.0
    com["exp28a"] = np.ascontiguousarray(e28[:, 0:128])
    com["exp28b"] = np.ascontiguousarray(e28[:, 128:256])

    nc = build_nc()
    in_maps = []
    for c in range(8):
        m = dict(com)
        m["xin"] = np.ascontiguousarray(xinT[:, c * 1024:(c + 1) * 1024]).reshape(-1)
        in_maps.append(m)
    res = run_bass_kernel_spmd(nc, in_maps, core_ids=list(range(8)))
    shards = [np.asarray(res.results[c]["out"]).reshape(4096, 1024) for c in range(8)]
    full = np.concatenate(shards, axis=1)
    return np.ascontiguousarray(full.T).reshape(4, 2048, 4096)


# version 07 (final)
# speedup vs baseline: 1.4210x; 1.4210x over previous
import os
import math
import numpy as np
import ml_dtypes
import concourse.bass as bass
import concourse.mybir as mybir
from concourse.bass_utils import run_bass_kernel_spmd

F32 = mybir.dt.float32
BF16 = mybir.dt.bfloat16
AF = mybir.ActivationFunctionType
ALU = mybir.AluOpType
AX = mybir.AxisListType

STAGE = int(os.environ.get("MK_STAGE", "4"))
SUB = os.environ.get("MK_SUB", "z")

D = 4096
CH = 32
TH = 512
NTOK = 1024
NH = 32
DH = 128
EPS = 1e-5
GW = 3
NSLOT = 4

R_MEAN = 0
R_A = 512
R_B = 1024
R_SC = 1536
R_QM = [2048, 2080]
R_KM = 2112
R_KA = 2116
R_BD = 2120
R_BD2 = 2152
R_R = 2560
R_DEF = [3072, 3584]


def build_nc():
    nc = bass.Bass("TRN2", target_bir_lowering=False,
                   detect_race_conditions=False)

    def dt_in(name, shape, dt=F32):
        return nc.dram_tensor(name, shape, dt, kind="ExternalInput")

    xin = dt_in("xin", [D * NTOK])
    wupT = dt_in("wupT", [D, D])
    l0aux = dt_in("l0aux", [D, 128])
    exb0 = dt_in("exb0", [4])
    lb0T = dt_in("lb0T", [128, D])
    wq = dt_in("wq", [D, D])
    bqT = dt_in("bqT", [128, 32])
    wk = dt_in("wk", [D, DH])
    bk = dt_in("bk", [128])
    wv = dt_in("wv", [D, DH])
    bv = dt_in("bv", [128])
    wo = dt_in("wo", [D, D])
    boT = dt_in("boT", [128, 32])
    bwT = dt_in("bwT", [D, D])
    g1T = dt_in("g1T", [D, 128])
    la1T = dt_in("la1T", [D, 256])
    lb1T = dt_in("lb1T", [256, D])
    lamd = dt_in("lam", [64, 2])
    pri = dt_in("pri", [64, 2])
    qri = dt_in("qri", [64, 2])
    vcreT = dt_in("vcreT", [64, 64])
    vcimT = dt_in("vcimT", [64, 64])
    bpd = dt_in("bp", [64, 1])
    ctri = dt_in("ctri", [64, 2])
    dD = dt_in("dD", [1])
    logstep = dt_in("logstep", [1])
    identf = dt_in("identf", [128, 128])
    onesb = dt_in("onesb", [128, 128], BF16)
    exp4 = dt_in("exp4", [4, 32])
    exp28a = dt_in("exp28a", [28, 128])
    exp28b = dt_in("exp28b", [28, 128])

    out = nc.dram_tensor("out", [D * NTOK], F32, kind="ExternalOutput")
    kpad = nc.dram_tensor("kpad", [8192], BF16)
    krep = nc.dram_tensor("krep", [128 * 8193], BF16)
    qspill = nc.dram_tensor("qspill", [NH * DH * NTOK], BF16)
    xspill = nc.dram_tensor("xspill", [2 * D * TH], F32)
    hrows = nc.dram_tensor("hrows", [64 * 512], F32)
    arf = nc.dram_tensor("arf", [2 * 4096], F32)
    ksums = nc.dram_tensor("ksums", [16 * 4096], F32)
    ccin = nc.dram_tensor("ccin", [2 * 128 * NTOK], BF16)
    ccout = nc.dram_tensor("ccout", [4 * 128 * NTOK], BF16)

    guards = []

    def sb(name, shape, dt=F32):
        t = nc.sbuf_tensor(name, shape, dt)
        h = t.__enter__()
        guards.append(t)
        return h

    XH = sb("XH", [128, CH * TH])
    PING = sb("PING", [128, CH * TH], BF16)
    PONG = sb("PONG", [128, CH * TH], BF16)
    KFM = sb("KFM", [128, 2048], BF16)
    VTM = sb("VTM", [128, 2048], BF16)
    WR = sb("WR", [128, 2 * 3072], BF16)
    TR = sb("TR", [128, 2 * 3072 + 2 * 512], BF16)
    SCF = sb("SCF", [128, 2048])
    SCB = sb("SCB", [128, 8 * 512], BF16)
    PT = sb("PT", [128, 4 * 512], BF16)
    QHB = sb("QHB", [128, 2 * 512], BF16)
    ABC = sb("ABC", [128, 512])
    BBC = sb("BBC", [128, 512])
    ROW = sb("ROW", [1, 512])
    RDEF = sb("RDEF", [128, 1024])
    QN = sb("QN", [128, 96])
    GT = sb("GT", [128, 2 * 512], BF16)
    H224 = sb("H224", [128, 2 * 512], BF16)
    NEGB = sb("NEGB", [128, 32])
    SMF = sb("SMF", [128, 64])
    CONS = sb("CONS", [128, 128], BF16)
    CONF = sb("CONF", [128, 480])
    LAMS = sb("LAMS", [64, 16])
    VST = sb("VST", [64, 8])
    OMB = sb("OMB", [128, 8 * 32])
    W64 = sb("W64", [64, 6 * 64])
    ARL = sb("ARL", [128, 16 * 32])
    S64 = sb("S64", [64, 8 * 64])
    KSB = sb("KSB", [64, 64])
    KTB = sb("KTB", [64, 64], BF16)

    def psb(name):
        t = nc.psum_tensor(name, [128, 512], F32)
        h = t.__enter__()
        guards.append(t)
        return h

    PS = [psb(f"PS{i}") for i in range(8)]

    PINGF = PING[:, :].bitcast(F32)
    PONGF = PONG[:, :].bitcast(F32)

    ONES = CONS[:, 0:128]
    C_E4 = CONF[0:4, 192:224]
    C_E28A = CONF[0:28, 224:352]
    C_E28B = CONF[0:28, 352:480]

    class Sem:
        def __init__(self, name):
            g = nc.semaphore(name)
            self.h = g.__enter__()
            guards.append(g)
            self.n = 0

        def inc(self, k=1):
            self.n += k
            return self.n

    swd = Sem("s_wd")
    std = Sem("s_td")
    sxd = Sem("s_xd")
    sqd = Sem("s_qd")
    sbc = Sem("s_bc")
    smm = Sem("s_mm")
    sev = Sem("s_ev")
    svv = Sem("s_vv")
    ssc = Sem("s_sc")
    sgp = Sem("s_gp")
    scc = Sem("s_cc")
    sst = Sem("s_st")
    sod = Sem("s_od")
    ssu = Sem("s_su")
    swu = Sem("s_wu")
    stu = Sem("s_tu")
    swsl = [Sem(f"s_ws{i}") for i in range(NSLOT)]
    stsl = [Sem(f"s_ts{i}") for i in range(NSLOT)]
    sqsl = [Sem(f"s_qs{i}") for i in range(2)]
    sxsl = [Sem(f"s_xs{i}") for i in range(NSLOT)]

    blk = nc.Block()
    block = blk.__enter__()

    class Prog:
        def __init__(self):
            self.q = {k: [] for k in ("tensor", "vector", "scalar", "gpsimd", "sync")}

        def on(self, eng, f):
            self.q[eng].append(f)

        def flush(self):
            for e in ("sync", "gpsimd", "tensor", "vector", "scalar"):
                fs = self.q[e]
                if fs:
                    def runner(obj, fs=tuple(fs)):
                        for f in fs:
                            f(obj)
                    getattr(block, e)(runner)
                    self.q[e] = []

    P = Prog()
    ENGS = ("tensor", "vector", "scalar", "gpsimd", "sync")

    BENGS = ("tensor", "vector", "scalar", "sync")

    def barrier():
        for e in BENGS:
            if e == "sync":
                P.on(e, lambda o: o.nop().then_inc(sst.h, 1))
            else:
                P.on(e, lambda o: o.drain().then_inc(sst.h, 1))
        sst.n += 4
        tgt = sst.n
        for e in BENGS:
            P.on(e, lambda o, t=tgt: o.wait_ge(sst.h, t))
        P.flush()

    hslot = [0]

    def bcast_rows(pairs, wait_sem=None, wait_val=None):
        base = sbc.n
        n = len(pairs)
        slots = [16 + (hslot[0] + i) % 48 for i in range(n)]
        hslot[0] += n
        sbc.n += 32 * n
        done = sbc.n

        def go(o):
            o.wait_ge(sbc.h, base)
            if wait_sem is not None:
                o.wait_ge(wait_sem, wait_val)
            for (src, dst, w), sl in zip(pairs, slots):
                o.dma_start(out=bass.AP(hrows, sl * 512, [[1, w]]),
                            in_=src).then_inc(sbc.h, 16)
            o.wait_ge(sbc.h, base + 16 * n)
            for (src, dst, w), sl in zip(pairs, slots):
                o.dma_start(out=dst,
                            in_=bass.AP(hrows, sl * 512, [[0, 128], [1, w]])
                            ).then_inc(sbc.h, 16)
        P.on("sync", go)
        return done

    def chunk(buf, c):
        return buf[:, c * TH:(c + 1) * TH]

    def rw(off, w=512):
        return ROW[0:1, off:off + w]

    def prologue():
        def load_consts(o):
            o.dma_start(out=CONF[:, 0:128], in_=identf[:, :]).then_inc(sxd.h, 16)
            o.dma_start(out=CONF[:, 128:160], in_=bqT[:, :]).then_inc(sxd.h, 16)
            o.dma_start(out=CONF[:, 160:192], in_=boT[:, :]).then_inc(sxd.h, 16)
            o.dma_start(out=ONES, in_=onesb[:, :]).then_inc(sxd.h, 16)
            o.dma_start(out=C_E4, in_=exp4[:, :]).then_inc(sxd.h, 16)
            o.dma_start(out=C_E28A, in_=exp28a[:, :]).then_inc(sxd.h, 16)
            o.dma_start(out=C_E28B, in_=exp28b[:, :]).then_inc(sxd.h, 16)
            o.dma_start(out=LAMS[:, 0:2], in_=lamd[:, :]).then_inc(sxd.h, 16)
            o.dma_start(out=LAMS[:, 2:4], in_=ctri[:, :]).then_inc(sxd.h, 16)
            o.dma_start(out=LAMS[:, 4:6], in_=pri[:, :]).then_inc(sxd.h, 16)
            o.dma_start(out=LAMS[:, 6:8], in_=qri[:, :]).then_inc(sxd.h, 16)
            o.dma_start(out=LAMS[:, 8:9], in_=bpd[:, :]).then_inc(sxd.h, 16)
            o.dma_start(out=SMF[0:1, 0:1], in_=dD[None, :]).then_inc(sxd.h, 16)
            o.dma_start(out=SMF[0:1, 1:2], in_=logstep[None, :]).then_inc(sxd.h, 16)
            o.dma_start(out=SMF[:, 32:33],
                        in_=bass.AP(bk, 0, [[1, 128], [1, 1]])).then_inc(sxd.h, 16)
            o.dma_start(out=SMF[:, 33:34],
                        in_=bass.AP(bv, 0, [[1, 128], [1, 1]])).then_inc(sxd.h, 16)
            o.dma_start(out=SMF[32:36, 40:41],
                        in_=bass.AP(exb0, 0, [[1, 4], [1, 1]])).then_inc(sxd.h, 16)
            o.dma_start(out=S64[:, 0:64], in_=vcreT[:, :]).then_inc(sxd.h, 16)
            o.dma_start(out=S64[:, 64:128], in_=vcimT[:, :]).then_inc(sxd.h, 16)
        sxd.n += 19 * 16
        consts_done = sxd.n
        P.on("sync", load_consts)

        def v_init(o):
            o.memset(ROW[:, :], 0.0)
            o.memset(RDEF[:, :], 0.0)
            o.memset(GT[:, :], 0.0)
            o.memset(H224[:, :], 0.0)
        P.on("vector", v_init)

        IOT, OMS, OMT = OMB[:, 0:32], OMB[:, 32:64], OMB[:, 64:96]
        SH, SF, OC = OMB[:, 96:128], OMB[:, 128:160], OMB[:, 160:192]
        GRE, GIM = OMB[:, 192:224], OMB[:, 224:256]

        P.on("gpsimd", lambda o: o.iota(IOT, [[1, 32]], channel_multiplier=32,
                                        allow_small_or_imprecise_dtypes=True
                                        ).then_inc(sgp.h, 1))
        g1 = sgp.inc()

        def v_theta(o, g1=g1):
            o.wait_ge(sgp.h, g1)
            o.tensor_single_scalar(OMS, IOT, 2048.0, ALU.is_ge)
            o.drain()
            o.scalar_tensor_tensor(OMT, OMS, -4096.0, IOT, ALU.mult, ALU.add
                                   ).then_inc(svv.h, 1)
        v1 = svv.inc()
        P.on("vector", v_theta)

        def s_trig(o, v1=v1, cd=consts_done):
            o.wait_ge(svv.h, v1)
            o.activation(SH, OMT, AF.Sin, scale=float(math.pi / 4096.0))
            o.activation(SF, OMT, AF.Sin, scale=float(2.0 * math.pi / 4096.0))
            o.wait_ge(sxd.h, cd)
            o.activation(SMF[0:1, 2:3], SMF[0:1, 1:2], AF.Exp).then_inc(ssc.h, 1)
        s1 = ssc.inc()
        P.on("scalar", s_trig)

        def v_gc(o, s1=s1):
            o.wait_ge(ssc.h, s1)
            o.tensor_copy(PONGF[:, 4096:4128], SH)
            o.tensor_copy(PONGF[:, 4128:4160], SF)
            o.tensor_copy(PONGF[:, 4160:4192], OMT)
            o.tensor_copy(PONGF[:, 4192:4224], IOT)
            o.reciprocal(SMF[0:1, 3:4], SMF[0:1, 2:3])
            o.drain()
            o.tensor_single_scalar(SMF[0:1, 3:4], SMF[0:1, 3:4], 2.0, ALU.mult)
            o.tensor_mul(OC, SH, SH)
            o.drain()
            o.tensor_scalar(OC, OC, -2.0, 1.0, ALU.mult, ALU.add)
            o.drain()
            o.tensor_scalar(OMS, OC, 1.0, None, ALU.add)
            o.drain()
            o.tensor_mul(OMS, OMS, OMS)
            o.tensor_mul(OMT, SF, SF)
            o.drain()
            o.tensor_add(OMS, OMS, OMT)
            o.drain()
            o.tensor_single_scalar(OMS, OMS, 1e-38, ALU.add)
            o.tensor_scalar(GRE, OC, -1.0, 1.0, ALU.mult, ALU.add)
            o.tensor_scalar(GIM, OC, 1.0, None, ALU.add)
            o.drain()
            o.tensor_mul(GRE, GRE, GIM)
            o.drain()
            o.tensor_sub(GRE, GRE, OMT)
            o.reciprocal(OMS, OMS)
            o.drain()
            o.tensor_mul(GRE, GRE, OMS)
            o.tensor_scalar(GIM, SF, 2.0, None, ALU.mult)
            o.drain()
            o.tensor_mul(GIM, GIM, OMS).then_inc(svv.h, 1)
        v2 = svv.inc()
        P.on("vector", v_gc)

        def tosb(o, v2=v2, base=sbc.n):
            o.wait_ge(svv.h, v2)
            o.dma_start(out=bass.AP(hrows, 15 * 512, [[1, 1]]),
                        in_=SMF[0:1, 3:4]).then_inc(sbc.h, 16)
            o.wait_ge(sbc.h, base + 16)
            o.dma_start(out=SMF[:, 4:5],
                        in_=bass.AP(hrows, 15 * 512, [[0, 128], [1, 1]])
                        ).then_inc(sbc.h, 16)
        sbc.n += 32
        tos_done = sbc.n
        P.on("sync", tosb)

        def v_gc2(o, t=tos_done):
            o.wait_ge(sbc.h, t)
            o.tensor_single_scalar(GRE, GRE, SMF[:, 4:5], ALU.mult)
            o.tensor_single_scalar(GIM, GIM, SMF[:, 4:5], ALU.mult)
            o.tensor_scalar(SH, OC, 1.0, None, ALU.add)
            o.drain()
            o.tensor_scalar(SH, SH, 2.0, None, ALU.mult)
            o.drain()
            o.tensor_mul(SH, SH, OMS)
            o.tensor_scalar(SF, SF, 2.0, None, ALU.mult)
            o.drain()
            o.tensor_mul(SF, SF, OMS).then_inc(svv.h, 1)
        v3 = svv.inc()
        P.on("vector", v_gc2)
        CRE, CIM = SH, SF

        def g_hbm(o, v3=v3):
            o.wait_ge(svv.h, v3)
            o.dma_start(out=bass.AP(hrows, 0, [[32, 128], [1, 32]]),
                        in_=GRE).then_inc(sbc.h, 16)
            o.dma_start(out=bass.AP(hrows, 4096, [[32, 128], [1, 32]]),
                        in_=GIM).then_inc(sbc.h, 16)
        sbc.n += 32
        ghbm_done = sbc.n
        P.on("sync", g_hbm)

        def t_B(o, cd=consts_done):
            o.wait_ge(sxd.h, cd)
            o.matmul(PS[0][0:64, 0:1], S64[:, 0:64], LAMS[:, 8:9],
                     start=True, stop=True)
            o.matmul(PS[0][0:64, 1:2], S64[:, 64:128], LAMS[:, 8:9],
                     start=True, stop=True).then_inc(smm.h, 1)
        m1 = smm.inc()
        P.on("tensor", t_B)

        def v_vst(o, m1=m1):
            o.wait_ge(smm.h, m1)
            cre, cim = LAMS[:, 2:3], LAMS[:, 3:4]
            pre, pim = LAMS[:, 4:5], LAMS[:, 5:6]
            qre, qim = LAMS[:, 6:7], LAMS[:, 7:8]
            bre, bim = PS[0][0:64, 0:1], PS[0][0:64, 1:2]
            last = None
            for i, (ar, ai, br, bi) in enumerate(
                    [(cre, cim, bre, bim), (cre, cim, pre, pim),
                     (qre, qim, bre, bim), (qre, qim, pre, pim)]):
                o.tensor_mul(VST[:, i:i + 1], ar, br)
                o.tensor_mul(LAMS[:, 9:10], ai, bi)
                o.drain()
                o.tensor_add(VST[:, i:i + 1], VST[:, i:i + 1], LAMS[:, 9:10])
                o.tensor_mul(VST[:, 4 + i:5 + i], ar, bi)
                o.drain()
                o.tensor_mul(LAMS[:, 9:10], ai, br)
                o.drain()
                last = o.tensor_sub(VST[:, 4 + i:5 + i], VST[:, 4 + i:5 + i],
                                    LAMS[:, 9:10])
                o.drain()
            last.then_inc(svv.h, 1)
        v4 = svv.inc()
        P.on("vector", v_vst)

        GREB = PINGF[0:64, 0:4096]
        GIMB = PINGF[0:64, 4096:8192]

        def gbc(o, t=ghbm_done):
            o.wait_ge(sbc.h, t)
            o.dma_start(out=GREB, in_=bass.AP(hrows, 0, [[0, 64], [1, 4096]])
                        ).then_inc(sbc.h, 16)
            o.dma_start(out=GIMB, in_=bass.AP(hrows, 4096, [[0, 64], [1, 4096]])
                        ).then_inc(sbc.h, 16)
        sbc.n += 32
        gbc_done = sbc.n
        P.on("sync", gbc)

        def v_inv(o, t=gbc_done):
            o.wait_ge(sbc.h, t)
            o.tensor_single_scalar(GREB, GREB, LAMS[:, 0:1], ALU.subtract)
            o.tensor_single_scalar(GIMB, GIMB, LAMS[:, 1:2], ALU.subtract)
            o.drain()
            last = None
            for it in range(8):
                dre = GREB[:, it * 512:(it + 1) * 512]
                dim = GIMB[:, it * 512:(it + 1) * 512]
                mag = ABC[0:64, :]
                o.tensor_mul(mag, dre, dre)
                o.tensor_mul(BBC[0:64, :], dim, dim)
                o.drain()
                o.tensor_add(mag, mag, BBC[0:64, :])
                o.drain()
                o.reciprocal(mag, mag)
                o.drain()
                o.tensor_mul(dre, dre, mag)
                o.tensor_mul(dim, dim, mag)
                o.drain()
                last = o.tensor_scalar(dim, dim, -1.0, None, ALU.mult)
                o.drain()
            last.then_inc(svv.h, 1)
        v5 = svv.inc()
        P.on("vector", v_inv)
        IRE, IIM = GREB, GIMB

        mm_base = smm.n
        ev_base = sev.n

        def t_cauchy(o, v4=v4, v5=v5, ev_base=ev_base):
            o.wait_ge(svv.h, max(v4, v5))
            for it in range(8):
                if it >= 1:
                    o.wait_ge(sev.h, ev_base + it)
                o.matmul(PS[1][0:8, 0:512], VST[:, :],
                         IRE[:, it * 512:(it + 1) * 512], start=True, stop=True)
                o.matmul(PS[2][0:8, 0:512], VST[:, :],
                         IIM[:, it * 512:(it + 1) * 512], start=True, stop=True
                         ).then_inc(smm.h, 1)
        smm.n += 8
        P.on("tensor", t_cauchy)

        KSO = PONGF

        def v_cev(o, mb=mm_base):
            for it in range(8):
                o.wait_ge(smm.h, mb + it + 1)
                o.tensor_copy(KSO[0:8, it * 512:(it + 1) * 512], PS[1][0:8, 0:512])
                o.tensor_copy(KSO[32:40, it * 512:(it + 1) * 512], PS[2][0:8, 0:512]
                              ).then_inc(sev.h, 1)
        sev.n += 8
        cev_done = sev.n
        P.on("vector", v_cev)

        def ksb(o, t=cev_done, base=sbc.n):
            o.wait_ge(sev.h, t)
            o.dma_start(out=bass.AP(ksums, 0, [[4096, 8], [1, 4096]]),
                        in_=KSO[0:8, 0:4096]).then_inc(sbc.h, 16)
            o.dma_start(out=bass.AP(ksums, 8 * 4096, [[4096, 8], [1, 4096]]),
                        in_=KSO[32:40, 0:4096]).then_inc(sbc.h, 16)
            o.wait_ge(sbc.h, base + 32)
            o.dma_start(out=ARL[:, :],
                        in_=bass.AP(ksums, 0, [[32, 128], [4096, 16], [1, 32]])
                        ).then_inc(sbc.h, 16)
        sbc.n += 48
        ks_done = sbc.n
        P.on("sync", ksb)

        def blkc(i):
            return ARL[:, i * 32:(i + 1) * 32]

        def sc(i):
            return PINGF[:, i * 32:(i + 1) * 32]

        def v_ar(o, t=ks_done):
            o.wait_ge(sbc.h, t)
            for v in range(4):
                o.tensor_sub(sc(v), blkc(v), blkc(12 + v))
                o.tensor_add(sc(4 + v), blkc(8 + v), blkc(4 + v))
            o.drain()
            o.tensor_scalar(sc(8), sc(3), 1.0, None, ALU.add)
            o.drain()
            o.tensor_mul(sc(9), sc(8), sc(8))
            o.tensor_mul(sc(10), sc(7), sc(7))
            o.drain()
            o.tensor_add(sc(9), sc(9), sc(10))
            o.drain()
            o.reciprocal(sc(9), sc(9))
            o.drain()
            o.tensor_mul(sc(8), sc(8), sc(9))
            o.tensor_mul(sc(10), sc(7), sc(9))
            o.drain()
            o.tensor_scalar(sc(10), sc(10), -1.0, None, ALU.mult)
            o.tensor_mul(sc(11), sc(1), sc(8))
            o.drain()
            o.tensor_mul(sc(12), sc(5), sc(10))
            o.drain()
            o.tensor_sub(sc(11), sc(11), sc(12))
            o.tensor_mul(sc(12), sc(1), sc(10))
            o.tensor_mul(sc(13), sc(5), sc(8))
            o.drain()
            o.tensor_add(sc(12), sc(12), sc(13))
            o.tensor_mul(sc(13), sc(11), sc(2))
            o.drain()
            o.tensor_mul(sc(14), sc(12), sc(6))
            o.drain()
            o.tensor_sub(sc(13), sc(13), sc(14))
            o.tensor_mul(sc(14), sc(11), sc(6))
            o.tensor_mul(sc(15), sc(12), sc(2))
            o.drain()
            o.tensor_add(sc(14), sc(14), sc(15))
            o.tensor_sub(sc(13), sc(0), sc(13))
            o.drain()
            o.tensor_sub(sc(14), sc(4), sc(14))
            o.tensor_mul(sc(15), CRE, sc(13))
            o.drain()
            o.tensor_mul(sc(16), CIM, sc(14))
            o.drain()
            o.tensor_sub(sc(15), sc(15), sc(16))
            o.tensor_mul(sc(16), CRE, sc(14))
            o.tensor_mul(sc(17), CIM, sc(13))
            o.drain()
            o.tensor_add(sc(16), sc(16), sc(17)).then_inc(svv.h, 1)
        v6 = svv.inc()
        P.on("vector", v_ar)

        def arb(o, t=v6, base=sbc.n):
            o.wait_ge(svv.h, t)
            o.dma_start(out=bass.AP(arf, 0, [[32, 128], [1, 32]]),
                        in_=sc(15)).then_inc(sbc.h, 16)
            o.dma_start(out=bass.AP(arf, 4096, [[32, 128], [1, 32]]),
                        in_=sc(16)).then_inc(sbc.h, 16)
            o.wait_ge(sbc.h, base + 32)
            o.dma_start(out=S64[:, 128:192],
                        in_=bass.AP(arf, 0, [[64, 64], [1, 64]])).then_inc(sbc.h, 16)
            o.dma_start(out=S64[:, 192:256],
                        in_=bass.AP(arf, 4096, [[64, 64], [1, 64]])).then_inc(sbc.h, 16)
        sbc.n += 64
        ar_done = sbc.n
        P.on("sync", arb)

        def g_i2(o, w1=v1):
            o.wait_ge(svv.h, w1)
            o.iota(W64[:, 320:384], [[1, 64]], channel_multiplier=0,
                   allow_small_or_imprecise_dtypes=True)
            o.iota(SMF[0:64, 16:17], [[1, 1]], channel_multiplier=1,
                   allow_small_or_imprecise_dtypes=True).then_inc(sgp.h, 1)
        g2 = sgp.inc()
        P.on("gpsimd", g_i2)

        def v_w64(o, g2=g2):
            o.wait_ge(sgp.h, g2)
            AC = S64[:, 256:320]
            o.tensor_single_scalar(AC, W64[:, 320:384], SMF[0:64, 16:17], ALU.mult)
            F = S64[:, 320:384]
            T = S64[:, 384:448]
            o.drain()
            o.tensor_single_scalar(F, AC, 1.0 / 64.0, ALU.mult)
            o.drain()
            for b in (32.0, 16.0, 8.0, 4.0, 2.0, 1.0):
                o.tensor_single_scalar(T, F, b, ALU.is_ge)
                o.drain()
                o.scalar_tensor_tensor(F, T, -b, F, ALU.mult, ALU.add)
                o.drain()
            o.tensor_single_scalar(T, F, 0.5, ALU.is_ge)
            o.drain()
            o.scalar_tensor_tensor(F, T, -1.0, F, ALU.mult, ALU.add)
            FT = S64[:, 448:512]
            o.tensor_single_scalar(FT, AC, 1.0 / 4096.0, ALU.mult)
            o.drain()
            o.tensor_single_scalar(T, FT, 0.5, ALU.is_ge)
            o.drain()
            o.scalar_tensor_tensor(FT, T, -1.0, FT, ALU.mult, ALU.add
                                   ).then_inc(svv.h, 1)
        v7 = svv.inc()
        P.on("vector", v_w64)

        def s_w64(o, v7=v7):
            o.wait_ge(svv.h, v7)
            F, FT = S64[:, 320:384], S64[:, 448:512]
            o.activation(W64[:, 64:128], F, AF.Sin, scale=float(2 * math.pi))
            o.activation(W64[:, 0:64], F, AF.Sin, scale=float(math.pi))
            o.activation(W64[:, 256:320], FT, AF.Sin, scale=float(2 * math.pi))
            o.activation(W64[:, 192:256], FT, AF.Sin, scale=float(math.pi)
                         ).then_inc(ssc.h, 1)
        s2 = ssc.inc()
        P.on("scalar", s_w64)

        def v_w64b(o, s2=s2, t=ar_done):
            o.wait_ge(ssc.h, s2)
            o.tensor_mul(W64[:, 0:64], W64[:, 0:64], W64[:, 0:64])
            o.drain()
            o.tensor_scalar(W64[:, 0:64], W64[:, 0:64], -2.0, 1.0, ALU.mult, ALU.add)
            o.tensor_mul(W64[:, 192:256], W64[:, 192:256], W64[:, 192:256])
            o.drain()
            o.tensor_scalar(W64[:, 192:256], W64[:, 192:256], -2.0, 1.0,
                            ALU.mult, ALU.add)
            o.tensor_scalar(W64[:, 128:192], W64[:, 64:128], -1.0, None, ALU.mult)
            o.wait_ge(sbc.h, t)
            o.tensor_scalar(S64[:, 256:320], S64[:, 192:256], -1.0, None, ALU.mult
                            ).then_inc(svv.h, 1)
        v8 = svv.inc()
        P.on("vector", v_w64b)

        def t_s1(o, v8=v8):
            o.wait_ge(svv.h, v8)
            arRe, arIm, nAr = S64[:, 128:192], S64[:, 192:256], S64[:, 256:320]
            wre, wim = W64[:, 0:64], W64[:, 64:128]
            o.matmul(PS[3][0:64, 0:64], arRe, wre, start=True, stop=False)
            o.matmul(PS[3][0:64, 0:64], nAr, wim, start=False, stop=True)
            o.matmul(PS[4][0:64, 0:64], arRe, wim, start=True, stop=False)
            o.matmul(PS[4][0:64, 0:64], arIm, wre, start=False, stop=True
                     ).then_inc(smm.h, 1)
        m2 = smm.inc()
        P.on("tensor", t_s1)

        def v_tw(o, m2=m2):
            o.wait_ge(smm.h, m2)
            SRE, SIM = S64[:, 320:384], S64[:, 384:448]
            o.tensor_copy(SRE, PS[3][0:64, 0:64])
            o.tensor_copy(SIM, PS[4][0:64, 0:64])
            o.drain()
            twre, twim = W64[:, 192:256], W64[:, 256:320]
            TRE, TIM = S64[:, 448:512], S64[:, 128:192]
            o.tensor_mul(TRE, SRE, twre)
            o.tensor_mul(KSB[:, :], SIM, twim)
            o.drain()
            o.tensor_sub(TRE, TRE, KSB[0:64, :])
            o.tensor_mul(TIM, SRE, twim)
            o.drain()
            o.tensor_mul(KSB[:, :], SIM, twre)
            o.drain()
            o.tensor_add(TIM, TIM, KSB[0:64, :])
            o.drain()
            o.tensor_scalar(S64[:, 192:256], TIM, -1.0, None, ALU.mult
                            ).then_inc(svv.h, 1)
        v9 = svv.inc()
        P.on("vector", v_tw)

        def t_s2(o, v9=v9):
            o.wait_ge(svv.h, v9)
            TRE, nTIM = S64[:, 448:512], S64[:, 192:256]
            o.matmul(PS[5][0:64, 0:64], TRE, W64[:, 0:64], start=True, stop=False)
            o.matmul(PS[5][0:64, 0:64], nTIM, W64[:, 64:128], start=False,
                     stop=True).then_inc(smm.h, 1)
        m3 = smm.inc()
        P.on("tensor", t_s2)

        def s_kev(o, m3=m3):
            o.wait_ge(smm.h, m3)
            o.mul(KSB[:, :], PS[5][0:64, 0:64], float(1.0 / 4096.0)).then_inc(ssc.h, 1)
        s3 = ssc.inc()
        P.on("scalar", s_kev)

        def v_kd(o, s3=s3):
            o.wait_ge(ssc.h, s3)
            o.tensor_add(KSB[0:1, 0:1], KSB[0:1, 0:1], SMF[0:1, 0:1]
                         ).then_inc(svv.h, 1)
        v10 = svv.inc()
        P.on("vector", v_kd)

        def t_ktr(o, v10=v10):
            o.wait_ge(svv.h, v10)
            o.transpose(PS[6][0:64, 0:64], KSB[:, :], CONF[0:64, 0:64]).then_inc(smm.h, 1)
        m4 = smm.inc()
        P.on("tensor", t_ktr)

        def v_ktb(o, m4=m4):
            o.wait_ge(smm.h, m4)
            o.memset(SCB[:, 0:64], 0.0)
            o.tensor_copy(KTB[:, :], PS[6][0:64, 0:64]).then_inc(svv.h, 1)
        v11 = svv.inc()
        P.on("vector", v_ktb)

        KREPSB = PONG[:, 0:8192]

        def kw(o, t=v11, base=sbc.n):
            o.wait_ge(svv.h, t)
            o.dma_start(out=bass.AP(kpad, 0, [[1, 8192]]),
                        in_=SCB[:, 0:64]).then_inc(sbc.h, 16)
            o.wait_ge(sbc.h, base + 16)
            o.dma_start(out=bass.AP(kpad, 4095, [[64, 64], [1, 64]]),
                        in_=KTB[:, :]).then_inc(sbc.h, 16)
            o.wait_ge(sbc.h, base + 32)
            o.dma_start(out=KREPSB, in_=bass.AP(kpad, 0, [[0, 128], [1, 8192]])
                        ).then_inc(sbc.h, 16)
            o.wait_ge(sbc.h, base + 48)
            o.dma_start(out=bass.AP(krep, 0, [[8193, 128], [1, 8192]]),
                        in_=KREPSB).then_inc(sbc.h, 16)
            o.wait_ge(sbc.h, base + 64)
        sbc.n += 64
        P.on("sync", kw)

    def ln(src_f32, dst_ping, is_post=False, y_src=None, half=0, upto="full",
           y_ready=None):
        for c in range(CH):
            if is_post:
                srcb = chunk(y_src, c)
            else:
                srcb = SCB[:, (c % 4) * 512:(c % 4 + 1) * 512]
            sq = SCB[:, (4 + c % 4) * 512:(5 + c % 4) * 512]

            cp_t = None
            if not is_post:
                def vcopy(o, c=c, srcb=srcb, wt=ssu.n - 3):
                    if c >= 4:
                        o.wait_ge(ssu.h, wt)
                    o.tensor_copy(srcb, chunk(src_f32, c)).then_inc(svv.h, 1)
                cp_t = svv.inc()
                P.on("vector", vcopy)

            def sstat(o, c=c, srcb=srcb, sq=sq, wt=ssu.n - 3, cp_t=cp_t,
                      yr=(y_ready[c] if (is_post and y_ready) else None)):
                if yr is not None:
                    o.wait_ge(sev.h, yr)
                if c >= 4:
                    o.wait_ge(ssu.h, wt)
                if cp_t is not None:
                    o.wait_ge(svv.h, cp_t)
                o.activation(sq, srcb, AF.Square).then_inc(ssc.h, 1)
            sq_t = ssc.inc()
            P.on("scalar", sstat)

            def tmm(o, c=c, srcb=srcb, sq=sq, sq_t=sq_t):
                o.wait_ge(ssc.h, sq_t)
                o.matmul(PS[7][:, :], ONES, sq,
                         start=(c == 0), stop=(c == CH - 1))
                o.matmul(PS[6][:, :], ONES, srcb,
                         start=(c == 0), stop=(c == CH - 1)).then_inc(ssu.h, 1)
                if c == CH - 1:
                    o.drain().then_inc(smm.h, 1)
            ssu.inc()
            P.on("tensor", tmm)
        stats_done = smm.inc()
        if upto == "stats":
            P.on("tensor", lambda o, t=stats_done: o.wait_ge(smm.h, t))
            return

        def vrows(o, t=stats_done):
            o.wait_ge(smm.h, t)
            o.tensor_single_scalar(ABC[:, :], PS[6][:, :], 1.0 / 4096.0, ALU.mult)
            o.tensor_single_scalar(BBC[:, :], PS[7][:, :], 1.0 / 4096.0, ALU.mult)
            o.drain()
            o.tensor_mul(SCF[:, 0:512], ABC[:, :], ABC[:, :])
            o.drain()
            o.tensor_sub(BBC[:, :], BBC[:, :], SCF[:, 0:512])
            o.drain()
            o.tensor_single_scalar(BBC[:, :], BBC[:, :], float(EPS), ALU.add
                                   ).then_inc(svv.h, 1)
        var_done = svv.inc()
        P.on("vector", vrows)
        if upto == "vrows":
            P.on("vector", lambda o, t=var_done: o.wait_ge(svv.h, t))
            return

        def srstd(o, t=var_done):
            o.wait_ge(svv.h, t)
            o.activation(SCF[:, 0:512], BBC[:, :], AF.Ln)
            o.drain()
            o.activation(BBC[:, :], SCF[:, 0:512], AF.Exp, scale=-0.5
                         ).then_inc(ssc.h, 1)
        rstd_done = ssc.inc()
        P.on("scalar", srstd)
        if upto == "srstd":
            P.on("scalar", lambda o, t=rstd_done: o.wait_ge(ssc.h, t))
            return

        def vbrow(o, t=rstd_done, post=is_post, half=half):
            o.wait_ge(ssc.h, t)
            last = o.tensor_mul(ABC[:, :], ABC[:, :], BBC[:, :])
            if post:
                o.drain()
                last = o.tensor_add(RDEF[:, half * 512:(half + 1) * 512],
                                    RDEF[:, half * 512:(half + 1) * 512],
                                    ABC[:, :])
            last.then_inc(svv.h, 1)
        brow_done = svv.inc()
        P.on("vector", vbrow)
        if upto == "rows":
            P.on("vector", lambda o, t=brow_done: o.wait_ge(svv.h, t))
            return
        P.on("vector", lambda o: o.drain())

        for c0 in range(0, CH, 2):
            if is_post:
                def vap(o, c0=c0):
                    t0 = SCF[:, (c0 % 4) * 512:(c0 % 4 + 1) * 512]
                    t1 = SCF[:, (c0 % 4 + 1) * 512:(c0 % 4 + 2) * 512]
                    o.tensor_mul(t0, chunk(y_src, c0), BBC[:, :])
                    o.tensor_mul(t1, chunk(y_src, c0 + 1), BBC[:, :])
                    o.drain()
                    o.tensor_add(chunk(src_f32, c0), chunk(src_f32, c0), t0
                                 ).then_inc(svv.h, 1)
                    o.tensor_add(chunk(src_f32, c0 + 1), chunk(src_f32, c0 + 1),
                                 t1).then_inc(svv.h, 1)
            else:
                def vap(o, c0=c0):
                    t0 = SCF[:, (c0 % 4) * 512:(c0 % 4 + 1) * 512]
                    t1 = SCF[:, (c0 % 4 + 1) * 512:(c0 % 4 + 2) * 512]
                    o.tensor_mul(t0, chunk(src_f32, c0), BBC[:, :])
                    o.tensor_mul(t1, chunk(src_f32, c0 + 1), BBC[:, :])
                    o.drain()
                    o.tensor_sub(chunk(dst_ping, c0), t0, ABC[:, :]
                                 ).then_inc(svv.h, 1)
                    o.tensor_sub(chunk(dst_ping, c0 + 1), t1, ABC[:, :]
                                 ).then_inc(svv.h, 1)
            svv.inc(2)
            P.on("vector", vap)
        return svv.n - CH

    def projection(w_dram, m_chunks, rhs_buf, evac, evac_eng="scalar",
                   tri=False, extra=None, rhs_ready=None, evac_guard=None):
        bank_sets = [(0, 1, 2), (3, 4, 5)]
        n_groups = (m_chunks + GW - 1) // GW
        ev_t = {}
        ev_entry = sev.n
        for g in range(n_groups):
            m0 = g * GW
            mw = min(GW, m_chunks - m0)
            banks = bank_sets[g % 2]
            ks = list(range(min(CH, m0 + mw))) if tri else list(range(CH))
            if g >= 2:
                P.on("tensor", lambda o, t=ev_t[g - 2]: o.wait_ge(sev.h, t))
            elif ev_entry > 0:
                P.on("tensor", lambda o, t=ev_entry: o.wait_ge(sev.h, t))
            SLAB = 8
            slabs = [ks[i:i + SLAB] for i in range(0, len(ks), SLAB)]
            ring = TR if tri else WR
            rel = stu if tri else swu
            for si, sl in enumerate(slabs):
                nsl = len(sl)
                if tri:
                    slot = (std.n // 16) % 2
                    ssem = stsl[slot]
                else:
                    slot = (swd.n // 16) % 2
                    ssem = swsl[slot]
                ub = rel.n - 1

                def load(o, sl=sl, nsl=nsl, m0=m0, mw=mw, slot=slot, ub=ub,
                         ssem=ssem, tri=tri, w=w_dram, rel=rel):
                    if ub > 0:
                        o.wait_ge(rel.h, ub)
                    if tri:
                        base = 4095 + m0 * 128 - 128 * sl[-1]
                        o.dma_start(
                            out=TR[:, slot * 3072:slot * 3072 + nsl * mw * 128],
                            in_=bass.AP(krep, base,
                                        [[8192, 128], [128, nsl], [1, mw * 128]])
                        ).then_inc(ssem.h, 16)
                    else:
                        o.dma_start(
                            out=WR[:, slot * 3072:slot * 3072 + nsl * mw * 128],
                            in_=bass.AP(w, sl[0] * 128 * D + m0 * 128,
                                        [[D, 128], [128 * D, nsl], [1, mw * 128]])
                        ).then_inc(ssem.h, 16)
                (std if tri else swd).inc(16)
                dma_t = ssem.inc(16)
                P.on("gpsimd", load)

                def mms(o, sl=sl, nsl=nsl, si=si, m0=m0, mw=mw, slot=slot,
                        banks=banks, dma_t=dma_t, tri=tri, nks=len(ks),
                        has_x=extra is not None, ssem=ssem, rel=rel, ring=ring,
                        g=g, rhs_ready=rhs_ready):
                    if rhs_ready is not None and (g == 0 or tri):
                        o.wait_ge(svv.h, rhs_ready + sl[-1] + 1)
                    o.wait_ge(ssem.h, dma_t)
                    last = None
                    for ki, k in enumerate(sl):
                        j = (sl[-1] - k) if tri else ki
                        tbase = slot * 3072 + j * mw * 128
                        for mi in range(mw):
                            m = m0 + mi
                            if tri and k > m:
                                continue
                            st = (k == 0)
                            sp = ((k == m) if tri else
                                  (k == nks - 1 and not has_x))
                            last = o.matmul(
                                PS[banks[mi]][:, :],
                                ring[:, tbase + mi * 128:tbase + (mi + 1) * 128],
                                rhs_buf[:, k * TH:(k + 1) * TH],
                                start=st, stop=sp)
                    last.then_inc(rel.h, 1)
                rel.inc()
                P.on("tensor", mms)
            if extra is not None:
                xdram, xparts = extra
                nxk = len(xparts)
                xoff = 0
                for xk, (rh, xrhs) in enumerate(xparts):
                    slot = (std.n // 16) % 2
                    ub = stu.n - 1
                    ssem = sxsl[slot]

                    def loadx(o, xoff=xoff, rh=rh, m0=m0, mw=mw, slot=slot,
                              ub=ub, xdram=xdram, ssem=ssem):
                        if ub > 0:
                            o.wait_ge(stu.h, ub)
                        o.dma_start(
                            out=TR[0:rh, 6144 + (slot % 2) * 512:
                                   6144 + (slot % 2) * 512 + mw * 128],
                            in_=xdram[xoff:xoff + rh,
                                      m0 * 128:m0 * 128 + mw * 128]
                        ).then_inc(ssem.h, 16)
                    std.inc(16)
                    dma_t = ssem.inc(16)
                    P.on("gpsimd", loadx)

                    def mmx(o, xk=xk, rh=rh, m0=m0, mw=mw, slot=slot,
                            banks=banks, dma_t=dma_t, xrhs=xrhs, nxk=nxk,
                            ssem=ssem):
                        o.wait_ge(ssem.h, dma_t)
                        last = None
                        for mi in range(mw):
                            tb = 6144 + (slot % 2) * 512 + mi * 128
                            last = o.matmul(PS[banks[mi]][:, :],
                                     TR[0:rh, tb:tb + 128],
                                     xrhs, start=False, stop=(xk == nxk - 1))
                        last.then_inc(stu.h, 1)
                    stu.inc()
                    P.on("tensor", mmx)
                    xoff += rh
            grp_done = smm.inc()
            P.on("tensor", lambda o: o.drain().then_inc(smm.h, 1))

            def ev(o, m0=m0, mw=mw, banks=banks, t=grp_done, g=g,
                   evac_guard=evac_guard):
                o.wait_ge(smm.h, t)
                if g == 0 and evac_guard is not None:
                    o.wait_ge(svv.h, evac_guard)
                last = None
                for mi in range(mw):
                    last = evac(o, m0 + mi, PS[banks[mi]][:, :])
                last.then_inc(sev.h, 1)
            ev_t[g] = sev.inc()
            P.on(evac_eng, ev)
        return {c: ev_t[c // GW] for c in range(m_chunks)}

    def thin_proj(w_dram, width, psum_ap, rhs_buf):
        rs = w_dram.shape[1]
        for k0 in range(0, CH, 8):
            nsl = min(8, CH - k0)
            slot = (swd.n // 16) % 2
            ub = swu.n - 1
            ssem = swsl[slot]

            def load(o, k0=k0, nsl=nsl, slot=slot, ub=ub, w=w_dram,
                     width=width, ssem=ssem, rs=rs):
                if ub > 0:
                    o.wait_ge(swu.h, ub)
                o.dma_start(out=WR[:, slot * 3072:slot * 3072 + nsl * width],
                            in_=bass.AP(w, k0 * 128 * rs,
                                        [[rs, 128], [128 * rs, nsl], [1, width]])
                            ).then_inc(ssem.h, 16)
            swd.inc(16)
            dma_t = ssem.inc(16)
            P.on("gpsimd", load)

            def mm(o, k0=k0, nsl=nsl, slot=slot, dma_t=dma_t, width=width,
                   ps=psum_ap, ssem=ssem):
                o.wait_ge(ssem.h, dma_t)
                last = None
                for j in range(nsl):
                    k = k0 + j
                    last = o.matmul(
                        ps, WR[:, slot * 3072 + j * width:slot * 3072 + (j + 1) * width],
                        rhs_buf[:, k * TH:(k + 1) * TH],
                        start=(k == 0), stop=(k == CH - 1))
                last.then_inc(swu.h, 1)
                if k0 + nsl == CH:
                    o.drain().then_inc(smm.h, 1)
            swu.inc()
            P.on("tensor", mm)
        return smm.inc()

    def load_xh(o_src_fn, pre_waits):
        def go(o):
            for s, v in pre_waits:
                o.wait_ge(s, v)
            o_src_fn(o)
        t = sxd.inc(16)
        P.on("sync", go)
        for e in ("vector", "scalar", "tensor"):
            P.on(e, lambda o, t=t: o.wait_ge(sxd.h, t))

    def evac_pong(o, m, psum):
        return o.copy(chunk(PONG, m), psum)

    def a_phase(h):
        pre = [(sxd.h, sxd.n), (sod.h, sod.n)]
        load_xh(lambda o, h=h: o.dma_start(
            out=XH[:, :],
            in_=bass.AP(xin, h * TH, [[NTOK, 128], [NTOK * 128, CH], [1, TH]])
        ).then_inc(sxd.h, 16), pre)
        P.on("scalar", lambda o, t=sqd.n: o.wait_ge(sqd.h, t))
        barrier()
        if STAGE == 1 and SUB == "pb":
            ln(XH, PING, half=h)
            barrier()
            projection(None, CH, PING, evac_pong, tri=True)
            barrier()
            for c in range(CH):
                def vd(o, c=c):
                    o.tensor_copy(chunk(XH, c), chunk(PONG, c)).then_inc(svv.h, 1)
                svv.inc()
                P.on("vector", vd)
            barrier()
            return
        if STAGE == 1 and SUB == "pa":
            ln(XH, PING, half=h)
            barrier()
            for c in range(CH):
                def vd(o, c=c):
                    o.tensor_copy(chunk(XH, c), chunk(PING, c)).then_inc(svv.h, 1)
                svv.inc()
                P.on("vector", vd)
            barrier()
            return
        if STAGE == 1 and SUB in ("s1", "s2", "s3", "v", "r"):
            ln(XH, PING, half=h,
               upto={"s1": "stats", "s2": "rows", "s3": "bcast",
                     "v": "vrows", "r": "srstd"}[SUB])
            barrier()
            return
        ap = ln(XH, PING, half=h)
        ev = projection(None, CH, PING, evac_pong, tri=True, rhs_ready=ap)
        ln(XH, None, is_post=True, y_src=PONG, half=h, y_ready=ev)
        if STAGE < 2:
            barrier()
            return
        ln(XH, PING, half=h)
        barrier()
        l0_ffn(h)
        barrier()
        ln(XH, None, is_post=True, y_src=PONG, half=h)
        barrier()
        if STAGE < 3:
            return
        ln(XH, PING, half=h)
        barrier()
        kvq(h)
        def spx(o, h=h):
            o.dma_start(out=bass.AP(xspill, h * D * TH, [[CH * TH, 128], [1, CH * TH]]),
                        in_=XH[:, :]).then_inc(sxd.h, 16)
        sxd.inc(16)
        P.on("scalar", spx)
        barrier()

    def l0_ffn(h):
        lg = thin_proj(l0aux, 128, PS[6][:, :], PING)

        def s_lg(o, t=lg):
            o.wait_ge(smm.h, t)
            o.activation(SCF[32:36, 0:512], PS[6][32:36, 0:512], AF.Identity,
                         bias=SMF[32:36, 40:41]).then_inc(ssc.h, 1)
        lge = ssc.inc()
        P.on("scalar", s_lg)
        barrier()
        for t4 in range(4):
            def t_tp(o, t4=t4, wv=svv.n - 1):
                if t4 >= 2:
                    o.wait_ge(svv.h, wv)
                o.transpose(PS[t4 % 2][0:128, 0:4],
                            SCF[32:36, t4 * 128:(t4 + 1) * 128],
                            CONF[32:36, 32:36]).then_inc(smm.h, 1)
            mt = smm.inc()
            P.on("tensor", t_tp)

            def v_m(o, t4=t4, mt=mt):
                o.wait_ge(smm.h, mt)
                lt = PS[t4 % 2][0:128, 0:4]
                o.reduce_max(SMF[:, 8:9], lt, axis=AX.X)
                o.drain()
                o.tensor_single_scalar(SCF[:, 512 + t4 * 4:512 + (t4 + 1) * 4],
                                       lt, SMF[:, 8:9], ALU.is_equal
                                       ).then_inc(svv.h, 1)
            svv.inc()
            P.on("vector", v_m)
        barrier()
        for t4 in range(4):
            def t_tb(o, t4=t4, wv=svv.n - 1):
                if t4 >= 2:
                    o.wait_ge(svv.h, wv)
                o.transpose(PS[t4 % 2][0:4, 0:128],
                            SCF[:, 512 + t4 * 4:512 + (t4 + 1) * 4],
                            CONF[0:128, 0:128]).then_inc(smm.h, 1)
            mt = smm.inc()
            P.on("tensor", t_tb)

            def v_me(o, t4=t4, mt=mt):
                o.wait_ge(smm.h, mt)
                o.tensor_copy(SCF[0:4, 1024 + t4 * 128:1024 + (t4 + 1) * 128],
                              PS[t4 % 2][0:4, 0:128]).then_inc(svv.h, 1)
            svv.inc()
            P.on("vector", v_me)
        barrier()
        def t_e4(o):
            o.matmul(PS[7][0:32, 0:512], C_E4, SCF[0:4, 1024:1536],
                     start=True, stop=True).then_inc(smm.h, 1)
        m32 = smm.inc()
        P.on("tensor", t_e4)

        def v_zh(o, m32=m32):
            o.wait_ge(smm.h, m32)
            o.tensor_copy(SCF[0:32, 512:1024], PS[7][0:32, 0:512])
            o.drain()
            o.tensor_mul(GT[0:32, 0:512], PS[6][0:32, 0:512], SCF[0:32, 512:1024]
                         ).then_inc(svv.h, 1)
        svv.inc()
        P.on("vector", v_zh)
        barrier()
        projection(wupT, CH, PING, evac_pong,
                   extra=(lb0T, [(128, GT[:, 0:512])]))

    def kvq(h):
        kt = thin_proj(wk, 128, PS[6][:, :], PING)

        def s_kev(o, t=kt):
            o.wait_ge(smm.h, t)
            o.activation(SCB[:, 0:512], PS[6][:, :], AF.Identity,
                         bias=SMF[:, 32:33]).then_inc(ssc.h, 1)
        ke = ssc.inc()
        P.on("scalar", s_kev)

        def d_k(o, h=h, ke=ke):
            o.wait_ge(ssc.h, ke)
            o.dma_start(out=bass.AP(ccin, h * 512, [[NTOK, 128], [1, 512]]),
                        in_=SCB[:, 0:512]).then_inc(sxd.h, 16)
        sxd.inc(16)
        P.on("sync", d_k)
        barrier()
        vt = thin_proj(wv, 128, PS[6][:, :], PING)

        def s_vev(o, t=vt):
            o.wait_ge(smm.h, t)
            o.activation(SCF[:, 0:512], PS[6][:, :], AF.Identity,
                         bias=SMF[:, 33:34]).then_inc(ssc.h, 1)
        ve = ssc.inc()
        P.on("scalar", s_vev)
        for t4 in range(4):
            def t_vt(o, t4=t4, ve=ve, wv=svv.n - 1):
                o.wait_ge(ssc.h, ve)
                if t4 >= 2:
                    o.wait_ge(svv.h, wv)
                o.transpose(PS[t4 % 2][0:128, 0:128],
                            SCF[:, t4 * 128:(t4 + 1) * 128],
                            CONF[0:128, 0:128]).then_inc(smm.h, 1)
            mt = smm.inc()
            P.on("tensor", t_vt)

            def v_vc(o, t4=t4, mt=mt):
                o.wait_ge(smm.h, mt)
                o.tensor_copy(SCB[:, 512 + t4 * 128:512 + (t4 + 1) * 128],
                              PS[t4 % 2][0:128, 0:128]).then_inc(svv.h, 1)
            svv.inc()
            P.on("vector", v_vc)
        vtc = svv.n

        def d_v(o, h=h, vtc=vtc):
            o.wait_ge(svv.h, vtc)
            o.dma_start(out=bass.AP(ccin, 128 * NTOK + h * 512 * 128,
                                    [[128, 128], [16384, 4], [1, 128]]),
                        in_=SCB[:, 512:1024]).then_inc(sxd.h, 16)
        sxd.inc(16)
        P.on("sync", d_v)
        barrier()

        def evac_q(o, m, psum):
            return o.activation(chunk(PONG, m), psum, AF.Identity,
                                bias=CONF[:, 128 + m:129 + m])
        projection(wq, CH, PING, evac_q)
        barrier()
        for m in range(CH):
            def t_qn(o, m=m, wv=svv.n - 1):
                if m >= 2:
                    o.wait_ge(svv.h, wv)
                o.matmul(PS[6 + m % 2][:, :], ONES, chunk(PONG, m),
                         start=True, stop=True).then_inc(smm.h, 1)
            mt = smm.inc()
            P.on("tensor", t_qn)

            def v_qn(o, m=m, h=h, mt=mt):
                o.wait_ge(smm.h, mt)
                o.reduce_max(QN[:, h * 32 + m:h * 32 + m + 1],
                             PS[6 + m % 2][:, :], axis=AX.X).then_inc(svv.h, 1)
            svv.inc()
            P.on("vector", v_qn)

        def d_q(o, h=h, wv=svv.n):
            o.wait_ge(svv.h, wv)
            o.dma_start(out=bass.AP(qspill, h * 512,
                                    [[NTOK, 128], [NTOK * 128, CH], [1, 512]]),
                        in_=PONG[:, :]).then_inc(sqd.h, 16)
        sqd.inc(16)
        P.on("sync", d_q)
        barrier()

    def exchange_and_prep():
        def g_cc(o, t=sxd.n):
            o.wait_ge(sxd.h, t)
            o.collective_compute(
                "AllGather", ALU.bypass,
                replica_groups=[[0, 1], [2, 3], [4, 5], [6, 7]],
                ins=[ccin.ap().opt()],
                outs=[ccout.ap().opt()],
            ).then_inc(scc.h, 1)
            o.wait_ge(scc.h, 1)
        scc.inc()
        P.on("gpsimd", g_cc)
        barrier()

        def d_kv(o, t=scc.n):
            o.wait_ge(scc.h, t)
            for r in range(2):
                o.dma_start(out=KFM[:, r * 1024:(r + 1) * 1024],
                            in_=bass.AP(ccout, r * 262144,
                                        [[1024, 128], [1, 1024]])).then_inc(sxd.h, 16)
                o.dma_start(out=VTM[:, r * 1024:(r + 1) * 1024],
                            in_=bass.AP(ccout, r * 262144 + 131072,
                                        [[128, 128], [16384, 8], [1, 128]])
                            ).then_inc(sxd.h, 16)
        t = sxd.inc(64)
        P.on("sync", d_kv)
        for e in ("vector", "scalar", "tensor"):
            P.on(e, lambda o, t=t: o.wait_ge(sxd.h, t))
        for t4 in range(4):
            def s_k2(o, t4=t4):
                o.activation(SCB[:, 0:512], KFM[:, t4 * 512:(t4 + 1) * 512],
                             AF.Square).then_inc(ssc.h, 1)
            st = ssc.inc()
            P.on("scalar", s_k2)

            def t_k2(o, st=st):
                o.wait_ge(ssc.h, st)
                o.matmul(PS[6][:, :], ONES, SCB[:, 0:512],
                         start=True, stop=True).then_inc(smm.h, 1)
            mt = smm.inc()
            P.on("tensor", t_k2)

            def v_k2(o, t4=t4, mt=mt):
                o.wait_ge(smm.h, mt)
                o.reduce_max(QN[:, 64 + t4:65 + t4], PS[6][:, :], axis=AX.X
                             ).then_inc(svv.h, 1)
            svv.inc()
            P.on("vector", v_k2)
            barrier()

        def v_bound(o):
            o.reduce_max(QN[:, 68:69], QN[:, 64:68], axis=AX.X)
            o.tensor_max(NEGB[:, :], QN[:, 0:32], QN[:, 32:64])
            o.drain()
            o.tensor_single_scalar(NEGB[:, :], NEGB[:, :], QN[:, 68:69],
                                   ALU.mult)
            o.drain()
            o.tensor_single_scalar(NEGB[:, :], NEGB[:, :], 1e-12, ALU.add
                                   ).then_inc(svv.h, 1)
        vb = svv.inc()
        P.on("vector", v_bound)

        def s_bound(o, vb=vb):
            o.wait_ge(svv.h, vb)
            o.activation(QN[:, 64:96], NEGB[:, :], AF.Ln)
            o.drain()
            o.activation(NEGB[:, :], QN[:, 64:96], AF.Exp, scale=0.5
                         ).then_inc(ssc.h, 1)
        sbd = ssc.inc()
        P.on("scalar", s_bound)

        def v_bound2(o, t=sbd):
            o.wait_ge(ssc.h, t)
            o.tensor_scalar(NEGB[:, :], NEGB[:, :],
                            float(-1.05 / math.sqrt(128.0)), -0.5,
                            ALU.mult, ALU.add).then_inc(svv.h, 1)
        svv.inc()
        P.on("vector", v_bound2)
        barrier()

    def b_phase(h):
        def d_qh(o, hd, h=h):
            o.dma_start(out=QHB[:, (hd % 2) * 512:(hd % 2 + 1) * 512],
                        in_=bass.AP(qspill, hd * DH * NTOK + h * 512,
                                    [[NTOK, 128], [1, 512]])
                        ).then_inc(sqsl[hd % 2].h, 16)

        P.on("sync", lambda o: d_qh(o, 0))
        q_t = {0: sqsl[0].inc(16)}
        qk_last = {}
        vr_t = {}
        vatt_t = {}
        exp_t = {}
        av_t = {}
        QB = (0, 1, 5)
        NG = NH * 16

        def emit_av(g):
            hd, kc = divmod(g, 16)

            def t_av(o, kc=kc, hd=hd, et=exp_t[g],
                     w_vr=vr_t.get(hd - 1), w_va=vatt_t.get(hd - 2)):
                o.wait_ge(ssc.h, et)
                if kc == 0:
                    if w_vr is not None:
                        o.wait_ge(svv.h, w_vr)
                    if w_va is not None:
                        o.wait_ge(svv.h, w_va)
                pt = PT[:, (kc % 4) * 512:(kc % 4 + 1) * 512]
                o.matmul(PS[4][:, :], ONES, pt,
                         start=(kc == 0), stop=(kc == 15))
                o.matmul(PS[2 + hd % 2][:, :], VTM[:, kc * 128:(kc + 1) * 128],
                         pt, start=(kc == 0), stop=(kc == 15)
                         ).then_inc(smm.h, 1)
            av_t[g] = smm.inc()
            P.on("tensor", t_av)
            if kc == 15:
                def v_r(o, t=av_t[g]):
                    o.wait_ge(smm.h, t)
                    o.reciprocal(ABC[:, :], PS[4][:, :]).then_inc(svv.h, 1)
                vr_t[hd] = svv.inc()
                P.on("vector", v_r)

                def v_att(o, hd=hd, vr=vr_t[hd]):
                    o.wait_ge(svv.h, vr)
                    o.drain()
                    o.tensor_mul(chunk(PING, hd), PS[2 + hd % 2][:, :],
                                 ABC[:, :]).then_inc(svv.h, 1)
                vatt_t[hd] = svv.inc()
                P.on("vector", v_att)

        for g in range(NG):
            hd, kc = divmod(g, 16)
            if kc == 0 and hd + 1 < NH:
                def pf(o, n=hd + 1, w=qk_last.get(hd - 1)):
                    if w is not None:
                        o.wait_ge(smm.h, w)
                    d_qh(o, n)
                P.on("sync", pf)
                q_t[hd + 1] = sqsl[(hd + 1) % 2].inc(16)

            def t_qk(o, g=g, kc=kc, hd=hd, qt=q_t[hd],
                     bw=(exp_t[g - 3] if g >= 3 else None)):
                if kc == 0:
                    o.wait_ge(sqsl[hd % 2].h, qt)
                if bw is not None:
                    o.wait_ge(ssc.h, bw)
                o.matmul(PS[QB[g % 3]][:, :], KFM[:, kc * 128:(kc + 1) * 128],
                         QHB[:, (hd % 2) * 512:(hd % 2 + 1) * 512],
                         start=True, stop=True).then_inc(smm.h, 1)
            qk_t = smm.inc()
            if kc == 15:
                qk_last[hd] = qk_t
            P.on("tensor", t_qk)

            def s_exp(o, g=g, kc=kc, hd=hd, qk_t=qk_t,
                      pw=(av_t[g - 4] if g >= 4 else None)):
                o.wait_ge(smm.h, qk_t)
                if pw is not None:
                    o.wait_ge(smm.h, pw)
                o.activation(PT[:, (kc % 4) * 512:(kc % 4 + 1) * 512],
                             PS[QB[g % 3]][:, :], AF.Exp,
                             bias=NEGB[:, hd:hd + 1],
                             scale=float(1.0 / math.sqrt(128.0))
                             ).then_inc(ssc.h, 1)
            exp_t[g] = ssc.inc()
            P.on("scalar", s_exp)

            if g >= 2:
                emit_av(g - 2)
        emit_av(NG - 2)
        emit_av(NG - 1)
        barrier()

    def c_phase(h):
        pre = [(sxd.h, sxd.n), (sod.h, sod.n)]
        load_xh(lambda o, h=h: o.dma_start(
            out=XH[:, :],
            in_=bass.AP(xspill, h * D * TH, [[CH * TH, 128], [1, CH * TH]])
        ).then_inc(sxd.h, 16), pre)
        barrier()

        def evac_wo(o, m, psum):
            return o.activation(chunk(PONG, m), psum, AF.Identity,
                                bias=CONF[:, 160 + m:161 + m])
        ev = projection(wo, CH, PING, evac_wo)
        ln(XH, None, is_post=True, y_src=PONG, half=h, y_ready=ev)
        if STAGE >= 4:
            ln(XH, PING, half=h)
            barrier()
            ev = l1_ffn(h)
            ln(XH, None, is_post=True, y_src=PONG, half=h, y_ready=ev)
            barrier()
        else:
            barrier()
        finish_half(h)

    def l1_ffn(h):
        lg = thin_proj(g1T, 128, PS[6][:, :], PING)

        def v_lg(o, t=lg):
            o.wait_ge(smm.h, t)
            o.tensor_copy(SCF[0:28, 0:512], PS[6][0:28, 0:512]).then_inc(svv.h, 1)
        svv.inc()
        P.on("vector", v_lg)
        barrier()
        for t4 in range(4):
            def t_tp(o, t4=t4, wv=svv.n - 1):
                if t4 >= 2:
                    o.wait_ge(svv.h, wv)
                o.transpose(PS[t4 % 2][0:128, 0:28],
                            SCF[0:28, t4 * 128:(t4 + 1) * 128],
                            CONF[0:28, 0:28]).then_inc(smm.h, 1)
            mt = smm.inc()
            P.on("tensor", t_tp)

            def v_t2(o, t4=t4, mt=mt):
                o.wait_ge(smm.h, mt)
                lt = PS[t4 % 2][0:128, 0:28]
                m1 = SCF[:, 512 + t4 * 28:512 + (t4 + 1) * 28]
                m2 = SCF[:, 624 + t4 * 28:624 + (t4 + 1) * 28]
                o.reduce_max(SMF[:, 8:9], lt, axis=AX.X)
                o.drain()
                o.tensor_single_scalar(m1, lt, SMF[:, 8:9], ALU.is_equal)
                o.drain()
                o.scalar_tensor_tensor(m2, m1, -1e30, lt, ALU.mult, ALU.add)
                o.drain()
                o.reduce_max(SMF[:, 12 + t4:13 + t4], m2, axis=AX.X)
                o.drain()
                o.tensor_single_scalar(m2, m2, SMF[:, 12 + t4:13 + t4],
                                       ALU.is_equal)
                o.tensor_sub(SMF[:, 16 + t4:17 + t4],
                             SMF[:, 12 + t4:13 + t4], SMF[:, 8:9]
                             ).then_inc(svv.h, 1)
            svv.inc()
            P.on("vector", v_t2)
        dv = svv.n

        def s_e2(o, dv=dv):
            o.wait_ge(svv.h, dv)
            o.activation(SMF[:, 20:24], SMF[:, 16:20], AF.Exp).then_inc(ssc.h, 1)
        se = ssc.inc()
        P.on("scalar", s_e2)

        def v_p(o, se=se):
            o.wait_ge(ssc.h, se)
            o.tensor_scalar(SMF[:, 24:28], SMF[:, 20:24], 1.0, None, ALU.add)
            o.drain()
            o.reciprocal(SMF[:, 24:28], SMF[:, 24:28])
            o.drain()
            o.tensor_mul(SMF[:, 28:32], SMF[:, 20:24], SMF[:, 24:28])
            o.drain()
            last = None
            for t4 in range(4):
                m1 = SCF[:, 512 + t4 * 28:512 + (t4 + 1) * 28]
                m2 = SCF[:, 624 + t4 * 28:624 + (t4 + 1) * 28]
                o.tensor_single_scalar(m1, m1, SMF[:, 24 + t4:25 + t4], ALU.mult)
                o.tensor_single_scalar(m2, m2, SMF[:, 28 + t4:29 + t4], ALU.mult)
                o.drain()
                last = o.tensor_add(m1, m1, m2)
                o.drain()
            last.then_inc(svv.h, 1)
        svv.inc()
        P.on("vector", v_p)
        barrier()
        for t4 in range(4):
            def t_gb(o, t4=t4, wv=svv.n - 1):
                if t4 >= 2:
                    o.wait_ge(svv.h, wv)
                o.transpose(PS[t4 % 2][0:28, 0:128],
                            SCF[:, 512 + t4 * 28:512 + (t4 + 1) * 28],
                            CONF[0:128, 0:128]).then_inc(smm.h, 1)
            mt = smm.inc()
            P.on("tensor", t_gb)

            def v_ge(o, t4=t4, mt=mt):
                o.wait_ge(smm.h, mt)
                o.tensor_copy(SCF[0:28, 1024 + t4 * 128:1024 + (t4 + 1) * 128],
                              PS[t4 % 2][0:28, 0:128]).then_inc(svv.h, 1)
            svv.inc()
            P.on("vector", v_ge)
        barrier()

        def t_e28(o):
            o.matmul(PS[6][:, :], C_E28A, SCF[0:28, 1024:1536],
                     start=True, stop=True)
            o.matmul(PS[7][:, :], C_E28B, SCF[0:28, 1024:1536],
                     start=True, stop=True).then_inc(smm.h, 1)
        me = smm.inc()
        P.on("tensor", t_e28)

        def v_g224(o, me=me):
            o.wait_ge(smm.h, me)
            o.tensor_copy(GT[:, 0:512], PS[6][:, :])
            o.tensor_copy(GT[:, 512:1024], PS[7][:, :]).then_inc(svv.h, 1)
        svv.inc()
        P.on("vector", v_g224)
        barrier()
        for k0 in range(0, CH, 8):
            nsl = min(8, CH - k0)
            slot = (swd.n // 16) % 2
            ub = swu.n - 1
            ssem = swsl[slot]

            def load(o, k0=k0, nsl=nsl, slot=slot, ub=ub, ssem=ssem):
                if ub > 0:
                    o.wait_ge(swu.h, ub)
                o.dma_start(out=WR[:, slot * 3072:slot * 3072 + nsl * 256],
                            in_=bass.AP(la1T, k0 * 128 * 256,
                                        [[256, 128], [128 * 256, nsl], [1, 256]])
                            ).then_inc(ssem.h, 16)
            swd.inc(16)
            dma_t = ssem.inc(16)
            P.on("gpsimd", load)

            def mm(o, k0=k0, nsl=nsl, slot=slot, dma_t=dma_t, ssem=ssem):
                o.wait_ge(ssem.h, dma_t)
                last = None
                for j in range(nsl):
                    k = k0 + j
                    tb = slot * 3072 + j * 256
                    o.matmul(PS[0][:, :], WR[:, tb:tb + 128],
                             PING[:, k * TH:(k + 1) * TH],
                             start=(k == 0), stop=(k == CH - 1))
                    last = o.matmul(PS[1][:, :], WR[:, tb + 128:tb + 256],
                                    PING[:, k * TH:(k + 1) * TH],
                                    start=(k == 0), stop=(k == CH - 1))
                last.then_inc(swu.h, 1)
                if k0 + nsl == CH:
                    o.drain().then_inc(smm.h, 1)
            swu.inc()
            P.on("tensor", mm)
        ht = smm.inc()

        def v_h224(o, t=ht):
            o.wait_ge(smm.h, t)
            o.tensor_mul(H224[:, 0:512], PS[0][:, :], GT[:, 0:512])
            o.tensor_mul(H224[:, 512:1024], PS[1][:, :],
                         GT[:, 512:1024]).then_inc(svv.h, 1)
        svv.inc()
        P.on("vector", v_h224)
        barrier()
        return projection(bwT, CH, PING, evac_pong,
                           extra=(lb1T, [(128, H224[:, 0:512]),
                                         (128, H224[:, 512:1024])]))

    def finish_half(h):
        for c in range(CH):
            def vf(o, c=c, h=h):
                o.tensor_sub(chunk(XH, c), chunk(XH, c),
                             RDEF[:, h * 512:(h + 1) * 512]).then_inc(svv.h, 1)
            svv.inc()
            P.on("vector", vf)
        fin = svv.n

        def wo_(o, h=h, fin=fin):
            o.wait_ge(svv.h, fin)
            o.dma_start(out=bass.AP(out, h * TH,
                                    [[NTOK, 128], [NTOK * 128, CH], [1, TH]]),
                        in_=XH[:, :]).then_inc(sod.h, 16)
        sod.inc(16)
        P.on("sync", wo_)
        barrier()

    prologue()
    P.on("gpsimd", lambda o, t=sbc.n: o.wait_ge(sbc.h, t))
    barrier()
    if STAGE == 9:
        def vdump(o):
            o.memset(XH[:, :], 0.0)
            o.tensor_copy(XH[:, 0:32], PINGF[:, 15 * 32:16 * 32])
            o.tensor_copy(XH[:, 32:64], PINGF[:, 16 * 32:17 * 32])
            o.tensor_copy(XH[:, 64:96], OMB[:, 192:224])
            o.tensor_copy(XH[:, 96:128], OMB[:, 224:256])
            o.tensor_copy(XH[0:64, 128:192], KSB[:, :])
            o.tensor_copy(XH[:, 192:704], ARL[:, :])
            o.tensor_copy(XH[:, 704:736], OMB[:, 96:128])
            o.tensor_copy(XH[:, 736:768], OMB[:, 128:160])
            o.tensor_copy(XH[:, 768:769], SMF[:, 4:5])
            o.tensor_copy(XH[:, 800:832], OMB[:, 32:64])
            o.tensor_copy(XH[:, 832:864], OMB[:, 160:192])
            o.tensor_copy(XH[:, 864:896], OMB[:, 64:96])
            o.tensor_copy(XH[:, 896:928], PONGF[:, 4096:4128])
            o.tensor_copy(XH[:, 928:960], PONGF[:, 4128:4160])
            o.tensor_copy(XH[:, 960:992], PONGF[:, 4160:4192])
            o.tensor_copy(XH[:, 992:1024], PONGF[:, 4192:4224])
            o.nop().then_inc(svv.h, 1)
        svv.inc()
        P.on("vector", vdump)
        barrier()
        finish_half(0)
        finish_half(1)
    elif STAGE == 0:
        for h in range(2):
            pre = [(sxd.h, sxd.n), (sod.h, sod.n)]
            load_xh(lambda o, h=h: o.dma_start(
                out=XH[:, :],
                in_=bass.AP(xin, h * TH, [[NTOK, 128], [NTOK * 128, CH], [1, TH]])
            ).then_inc(sxd.h, 16), pre)
            barrier()
            finish_half(h)
    elif STAGE < 3:
        a_phase(0)
        finish_half(0)
        a_phase(1)
        finish_half(1)
    else:
        a_phase(0)
        a_phase(1)
        exchange_and_prep()
        b_phase(0)
        c_phase(0)
        b_phase(1)
        c_phase(1)

    P.on("sync", lambda o, t=sod.n: o.wait_ge(sod.h, t))
    P.flush()

    blk.__exit__(None, None, None)
    for g in reversed(guards):
        g.__exit__(None, None, None)
    return nc


def kernel(**inputs):
    f32 = np.float32

    def T(a):
        return np.ascontiguousarray(np.asarray(a, dtype=f32).T)

    u = np.asarray(inputs["L0_attn_u"], dtype=f32).reshape(8192, 4096)
    xinT = np.ascontiguousarray(u.T)

    com = {}
    com["wupT"] = T(inputs["L0_ffn_W_up"])
    l0aux = np.zeros((4096, 128), dtype=f32)
    l0aux[:, 0:32] = np.asarray(inputs["L0_ffn_lora_A"], dtype=f32).reshape(32, 4096).T
    l0aux[:, 32:36] = np.asarray(inputs["L0_ffn_gate_w"], dtype=f32).T
    com["l0aux"] = l0aux
    com["exb0"] = np.asarray(inputs["L0_ffn_expert_bias"], dtype=f32)
    lb0 = np.zeros((128, 4096), dtype=f32)
    lb0[0:32] = np.asarray(inputs["L0_ffn_lora_B"], dtype=f32).transpose(0, 2, 1).reshape(32, 4096)
    com["lb0T"] = lb0
    com["wq"] = np.ascontiguousarray(np.asarray(inputs["L1_attn_Wq"], dtype=f32))
    com["bqT"] = np.ascontiguousarray(
        np.asarray(inputs["L1_attn_bq"], dtype=f32).reshape(32, 128).T)
    com["wk"] = np.ascontiguousarray(np.asarray(inputs["L1_attn_Wk"], dtype=f32))
    com["bk"] = np.asarray(inputs["L1_attn_bk"], dtype=f32)
    com["wv"] = np.ascontiguousarray(np.asarray(inputs["L1_attn_Wv"], dtype=f32))
    com["bv"] = np.asarray(inputs["L1_attn_bv"], dtype=f32)
    com["wo"] = np.ascontiguousarray(np.asarray(inputs["L1_attn_Wo"], dtype=f32))
    com["boT"] = np.ascontiguousarray(
        np.asarray(inputs["L1_attn_bo"], dtype=f32).reshape(32, 128).T)
    com["bwT"] = T(inputs["L1_ffn_base_W"])
    g1p = np.zeros((4096, 128), dtype=f32)
    g1p[:, 0:28] = np.asarray(inputs["L1_ffn_gate_W"], dtype=f32).T
    com["g1T"] = g1p
    la1p = np.zeros((4096, 256), dtype=f32)
    la1p[:, 0:224] = np.asarray(inputs["L1_ffn_lora_A"], dtype=f32).reshape(224, 4096).T
    com["la1T"] = la1p
    lb1p = np.zeros((256, 4096), dtype=f32)
    lb1p[0:224] = np.asarray(inputs["L1_ffn_lora_B"], dtype=f32).transpose(0, 2, 1).reshape(224, 4096)
    com["lb1T"] = lb1p
    com["lam"] = np.asarray(inputs["L0_attn_Lambda_ri"], dtype=f32)
    com["pri"] = np.asarray(inputs["L0_attn_p_ri"], dtype=f32)
    com["qri"] = np.asarray(inputs["L0_attn_q_ri"], dtype=f32)
    vc = np.asarray(inputs["L0_attn_Vc_ri"], dtype=f32)
    com["vcreT"] = np.ascontiguousarray(vc[:, :, 0].T)
    com["vcimT"] = np.ascontiguousarray(vc[:, :, 1].T)
    com["bp"] = np.asarray(inputs["L0_attn_Bp"], dtype=f32)
    com["ctri"] = np.asarray(inputs["L0_attn_Ct_ri"], dtype=f32).reshape(64, 2)
    com["dD"] = np.asarray(inputs["L0_attn_D"], dtype=f32)
    com["logstep"] = np.asarray(inputs["L0_attn_log_step"], dtype=f32)
    com["identf"] = np.eye(128, dtype=f32)
    com["onesb"] = np.ones((128, 128), dtype=ml_dtypes.bfloat16)
    e4 = np.zeros((4, 32), dtype=f32)
    for e in range(4):
        e4[e, e * 8:(e + 1) * 8] = 1.0
    com["exp4"] = e4
    e28 = np.zeros((28, 256), dtype=f32)
    for e in range(28):
        e28[e, e * 8:(e + 1) * 8] = 2.0
    com["exp28a"] = np.ascontiguousarray(e28[:, 0:128])
    com["exp28b"] = np.ascontiguousarray(e28[:, 128:256])

    nc = build_nc()
    in_maps = []
    for c in range(8):
        m = dict(com)
        m["xin"] = np.ascontiguousarray(xinT[:, c * 1024:(c + 1) * 1024]).reshape(-1)
        in_maps.append(m)
    res = run_bass_kernel_spmd(nc, in_maps, core_ids=list(range(8)))
    shards = [np.asarray(res.results[c]["out"]).reshape(4096, 1024) for c in range(8)]
    full = np.concatenate(shards, axis=1)
    return np.ascontiguousarray(full.T).reshape(4, 2048, 4096)
